# version 03
# baseline (speedup 1.0000x reference)
import numpy as np
import ml_dtypes
import concourse.bass as bass
import concourse.mybir as mybir
from concourse.bass_utils import run_bass_kernel_spmd

F32 = mybir.dt.float32
BF16 = mybir.dt.bfloat16
AF = mybir.ActivationFunctionType
ALU = mybir.AluOpType
AX = mybir.AxisListType

D = 2048
T = 4096
H = 16
DH = 128
FF = 5632
NKD = D // 128
NMQKV = 3 * D // 128
NFF = FF // 128
NTT = T // 128
SCALE = float(1.0 / np.sqrt(DH))
EPS = 1e-5
QEPS = 1e-8
CLAMP = 1e-10
NE_FF = FF * D

BF = ml_dtypes.bfloat16
DEBUG_TAPS = ()
P1_MM = True
P1_GAMMA = True
P1_DBGXN = False
P1_GFIN = True
P1_GLATE = True
P1_GGP = False


def build(debug_taps=(), phases=(0, 1, 2, 3, 35, 4)):
    nc = bass.Bass("TRN2", target_bir_lowering=False)

    x_in = nc.dram_tensor("x", [T, D], F32, kind="ExternalInput")
    ipw = nc.dram_tensor("ipw", [3 * D, D], F32, kind="ExternalInput")
    ipb = nc.dram_tensor("ipb", [3 * D], F32, kind="ExternalInput")
    oww = nc.dram_tensor("oww", [D, D], F32, kind="ExternalInput")
    owb = nc.dram_tensor("owb", [D], F32, kind="ExternalInput")
    alibi = nc.dram_tensor("alibi", [H], F32, kind="ExternalInput")
    w1 = nc.dram_tensor("w1", [FF, D], F32, kind="ExternalInput")
    w2 = nc.dram_tensor("w2", [FF, D], F32, kind="ExternalInput")
    w3 = nc.dram_tensor("w3", [D, FF], F32, kind="ExternalInput")
    c_identbf = nc.dram_tensor("c_identbf", [128, 128], BF16, kind="ExternalInput")
    c_identf = nc.dram_tensor("c_identf", [128, 128], F32, kind="ExternalInput")
    c_dbig = nc.dram_tensor("c_dbig", [128, 32], F32, kind="ExternalInput")
    c_diagkeep = nc.dram_tensor("c_diagkeep", [128, 128], BF16, kind="ExternalInput")
    c_rbig = nc.dram_tensor("c_rbig", [128, 4480], F32, kind="ExternalInput")

    y_out = nc.dram_tensor("y", [T, D], F32, kind="ExternalOutput")

    IPT3 = nc.dram_tensor("IPT3", [NMQKV, 128, NKD, 128], BF16, kind="Internal")
    OWT3 = nc.dram_tensor("OWT3", [NKD, 128, NKD, 128], BF16, kind="Internal")
    W1T3 = nc.dram_tensor("W1T3", [NFF, 128, NKD, 128], BF16, kind="Internal")
    W2T3 = nc.dram_tensor("W2T3", [NFF, 128, NKD, 128], BF16, kind="Internal")
    W3T3 = nc.dram_tensor("W3T3", [NKD, 128, NFF, 128], BF16, kind="Internal")
    IPBF = nc.dram_tensor("IPBF", [3 * D, D], BF16, kind="Internal")
    OWBF = nc.dram_tensor("OWBF", [D, D], BF16, kind="Internal")
    QS = nc.dram_tensor("QS", [H, 128, T], BF16, kind="Internal")
    KS = nc.dram_tensor("KS", [H, 128, T], BF16, kind="Internal")
    VS = nc.dram_tensor("VS", [H, 128, T], BF16, kind="Internal")
    ATT = nc.dram_tensor("ATT", [H, 128, T], BF16, kind="Internal")
    X2S = nc.dram_tensor("X2S", [T, D], F32, kind="Internal")
    XN2S = nc.dram_tensor("XN2S", [NKD, 128, T], BF16, kind="Internal")
    XN1DBG = nc.dram_tensor("XN1DBG", [NKD, 128, T], BF16, kind="Internal")
    scratch = {"QS": QS, "KS": KS, "VS": VS, "ATT": ATT, "X2S": X2S, "XN2S": XN2S,
               "IPT3": IPT3, "OWT3": OWT3, "W1T3": W1T3, "W2T3": W2T3, "W3T3": W3T3,
               "IPBF": IPBF, "XN1DBG": XN1DBG}

    identbf = nc.alloc_sbuf_tensor("identbf", [128, 128], BF16)
    identf = nc.alloc_sbuf_tensor("identf", [128, 128], F32)
    diagkeep = nc.alloc_sbuf_tensor("diagkeep", [128, 128], BF16)
    ones_bf = nc.alloc_sbuf_tensor("ones_bf", [128, 128], BF16)
    ones_f = nc.alloc_sbuf_tensor("ones_f", [128, 128], F32)
    epscol = nc.alloc_sbuf_tensor("epscol", [128, 1], F32)
    bWcol = nc.alloc_sbuf_tensor("bWcol", [128, 64], F32)
    slopes_bc = nc.alloc_sbuf_tensor("slopes_bc", [128, H], F32)
    jtab = nc.alloc_sbuf_tensor("jtab", [128, H * 32], F32)
    gthr = nc.alloc_sbuf_tensor("gthr", [128, 6], F32)
    onesrow = nc.alloc_sbuf_tensor("onesrow", [1, 128], F32)

    sem_names = [
        "c_ld", "p0_ldi", "p0_ldo", "w_ld0", "w_ld1", "p0_pe", "p0_act",
        "p0_dve", "p0_st0", "p0_st1",
        "x_ld0", "x_ld1", "ln_act", "ln_dve", "tr_pe", "fm_act",
        "wt_ld0", "wt_ld1", "mm_pe", "ev_act",
        "qst0", "qst1", "qst2", "qst3", "gm_ld0", "gm_ld1",
        "gm_dve", "gm_pe", "ch",
    ]
    S = {n: nc.alloc_semaphore(n) for n in sem_names}
    _ch = {"n": 0}

    def edge2(v, ins):
        _ch["n"] += 1
        ins.then_inc(S["ch"], 1)
        v.wait_ge(S["ch"], _ch["n"])

    NW = NMQKV + NKD

    def phase0():
        with (
          nc.sbuf_tensor("p0_btile", [128, 128], F32) as btile,
          nc.sbuf_tensor("p0_arow", [1, 128], F32) as arow,
          nc.sbuf_tensor("p0_onesf1", [1, 128], F32) as onesf1,
          nc.sbuf_tensor("p0_wstrip", [128, 2, D], BF16) as wstrip,
          nc.sbuf_tensor("p0_ostrip", [128, 2, D], BF16) as ostrip,
          nc.sbuf_tensor("p0_dbig", [128, 32], F32) as dbig,
          nc.psum_tensor("p0_ps", [128, 512], F32) as p0ps,
          nc.psum_tensor("p0_pst0", [128, 2048], BF16) as p0pst0,
          nc.psum_tensor("p0_pst1", [128, 2048], BF16) as p0pst1,
          nc.Block() as blk,
      ):
          p0pst = [p0pst0, p0pst1]

          @blk.gpsimd
          def _(g):
              for i in range(6):
                  g.dma_start(IPBF.ap()[i * 1024:(i + 1) * 1024, :],
                              ipw.ap()[i * 1024:(i + 1) * 1024, :]).then_inc(S["p0_ldi"], 16)
              for i in range(2):
                  g.dma_start(OWBF.ap()[i * 1024:(i + 1) * 1024, :],
                              oww.ap()[i * 1024:(i + 1) * 1024, :]).then_inc(S["p0_ldo"], 16)
              for j in range(NW):
                  if j < NMQKV:
                      src = IPBF.ap()[j * 128:(j + 1) * 128, :]
                      g.wait_ge(S["p0_ldi"], 16 * 6)
                  else:
                      src = OWBF.ap()[(j - NMQKV) * 128:(j - NMQKV + 1) * 128, :]
                      g.wait_ge(S["p0_ldo"], 16 * 2)
                  if j >= 2:
                      g.wait_ge(S["p0_pe"], j)
                  g.dma_start(wstrip[:, j % 2, :], src).then_inc(S["w_ld%d" % (j % 2)], 16)

          @blk.sync
          def _(sync):
              sync.dma_start(identbf[:], c_identbf.ap()).then_inc(S["c_ld"], 16)
              sync.dma_start(identf[:], c_identf.ap()).then_inc(S["c_ld"], 16)
              sync.dma_start(diagkeep[:], c_diagkeep.ap()).then_inc(S["c_ld"], 16)
              sync.dma_start(dbig[:], c_dbig.ap()).then_inc(S["c_ld"], 16)
              sync.dma_start(btile[0:48, :],
                             ipb.ap().rearrange("(c p) -> c p", p=128)).then_inc(S["c_ld"], 16)
              sync.dma_start(btile[48:64, :],
                             owb.ap().rearrange("(c p) -> c p", p=128)).then_inc(S["c_ld"], 16)
              sync.dma_start(arow[0:1, 0:H],
                             alibi.ap().rearrange("(o h) -> o h", o=1)).then_inc(S["c_ld"], 16)
              for j in range(NW):
                  sync.wait_ge(S["p0_act"], 2 + j)
                  dst = IPT3.ap()[j] if j < NMQKV else OWT3.ap()[j - NMQKV]
                  sync.dma_start(dst.rearrange("p kc ml -> p (kc ml)"),
                                 ostrip[:, j % 2, :]).then_inc(S["p0_st%d" % (j % 2)], 16)
              sync.wait_ge(S["p0_st0"], 16 * (NW // 2))
              sync.wait_ge(S["p0_st1"], 16 * (NW // 2))

          @blk.vector
          def _(v):
              v.memset(ones_bf[:], 1.0)
              v.memset(ones_f[:], 1.0)
              v.memset(epscol[:], EPS)
              v.memset(onesf1[:], 1.0)
              v.memset(onesrow[:], 1.0)
              v.wait_ge(S["c_ld"], 16 * 7)
              v.memset(btile[64:128, :], 0.0).then_inc(S["p0_dve"], 1)
              v.wait_ge(S["p0_act"], 1)
              for h in range(H):
                  v.tensor_scalar(jtab[:, h * 32:(h + 1) * 32], dbig[:],
                                  slopes_bc[:, h:h + 1], None, ALU.mult).then_inc(S["p0_dve"], 1)

          @blk.tensor
          def _(t):
              t.wait_ge(S["c_ld"], 16 * 7)
              t.wait_ge(S["p0_dve"], 1)
              t.transpose(p0ps[:, 0:128], btile[:], identf[:])
              t.matmul(p0ps[:, 128:128 + H], onesf1[0:1, :], arow[0:1, 0:H],
                       start=True, stop=True).then_inc(S["p0_pe"], 1)
              for j in range(NW):
                  t.wait_ge(S["w_ld%d" % (j % 2)], 16 * (j // 2 + 1))
                  if j >= 2:
                      t.wait_ge(S["p0_act"], j)
                  ins = None
                  for kc in range(NKD):
                      ins = t.transpose(p0pst[j % 2][:, kc * 128:(kc + 1) * 128],
                                        wstrip[:, j % 2, kc * 128:(kc + 1) * 128], identbf[:])
                  ins.then_inc(S["p0_pe"], 1)

          @blk.scalar
          def _(sc):
              sc.wait_ge(S["p0_pe"], 1)
              sc.mul(bWcol[:, 0:16], p0ps[:, 0:16], SCALE)
              sc.copy(bWcol[:, 16:64], p0ps[:, 16:64])
              sc.copy(slopes_bc[:], p0ps[:, 128:128 + H]).then_inc(S["p0_act"], 1)
              for j in range(NW):
                  sc.wait_ge(S["p0_pe"], 2 + j)
                  if j >= 2:
                      sc.wait_ge(S["p0_st%d" % (j % 2)], 16 * (j // 2))
                  sc.copy(ostrip[:, j % 2, :], p0pst[j % 2][:, :]).then_inc(S["p0_act"], 1)

    def phase1():
      gpieces = []
      for wten in (w1, w2):
          for r in range(FF // 128):
              gpieces.append((wten, r * 128, 0, D))
      for r in range(D // 128):
          for c0, cn in ((0, 2048), (2048, 2048), (4096, 1536)):
              gpieces.append((w3, r * 128, c0, cn))
      NG = len(gpieces)
      GW1, GW2 = NFF, 2 * NFF
      gsplit = [int(np.ceil(NG * k / NTT)) for k in range(NTT + 1)]
      gl_per_q = [(NG * (q + 1) + 3) // 4 - (NG * q + 3) // 4 for q in range(4)]

      with (
          nc.sbuf_tensor("p1_x", [128, 2, D], F32) as xbuf,
          nc.sbuf_tensor("p1_sq", [128, D], F32) as sqbuf,
          nc.sbuf_tensor("p1_st", [128, 2, 8], F32) as stat,
          nc.sbuf_tensor("p1_xn", [128, 8, D], BF16) as xntm,
          nc.sbuf_tensor("p1_fm", [128, 2, NKD * 1024], BF16) as xnfm,
          nc.sbuf_tensor("p1_wt", [128, 2, NKD * 128], BF16) as wtile,
          nc.sbuf_tensor("p1_ev", [128, 4, 512], BF16) as evbuf,
          nc.sbuf_tensor("p1_gb", [128, 2, D], F32) as gbuf,
          nc.sbuf_tensor("p1_gc", [128, 140], F32) as gcols,
          nc.psum_tensor("p1_mm0", [128, 512], F32) as mm0,
          nc.psum_tensor("p1_mm1", [128, 512], F32) as mm1,
          nc.psum_tensor("p1_mm2", [128, 512], F32) as mm2,
          nc.psum_tensor("p1_mm3", [128, 512], F32) as mm3,
          nc.psum_tensor("p1_pt0", [128, 2048], BF16) as pt0,
          nc.psum_tensor("p1_pt1", [128, 2048], BF16) as pt1,
          nc.Block() as blk,
      ):
          mmb = [mm0, mm1, mm2, mm3]
          ptb = [pt0, pt1]

          def ev_dst(E):
              m = (E // 2) % NMQKV
              Q = E // 96
              half = E % 2
              h = m % 16
              ten = QS if m < 16 else (KS if m < 32 else VS)
              off = Q * 1024 + half * 512
              return ten.ap()[h][:, off:off + 512]

          @blk.sync
          def _(sync):
              gl = 0
              for Q in range(4):
                  for m in range(NMQKV):
                      gidx = Q * NMQKV + m
                      if P1_MM:
                          if gidx >= 2:
                              sync.wait_ge(S["mm_pe"], 2 * (gidx - 1))
                          sync.dma_start(wtile[:, gidx % 2, :],
                                         IPT3.ap()[m].rearrange("p kc ml -> p (kc ml)")
                                         ).then_inc(S["wt_ld%d" % (gidx % 2)], 16)
                      if P1_GAMMA and (not P1_GLATE) and (not P1_GGP) and m % 2 == 0 and gl < (NG * (Q + 1) + 3) // 4 and gl < NG:
                          wten, r0, c0, cn = gpieces[gl]
                          if gl >= 2:
                              sync.wait_ge(S["gm_dve"], gl - 1)
                          sync.dma_start(gbuf[:, gl % 2, 0:cn],
                                         wten.ap()[r0:r0 + 128, c0:c0 + cn]
                                         ).then_inc(S["gm_ld%d" % (gl % 2)], 16)
                          gl += 1
              while P1_GAMMA and (not P1_GGP) and gl < NG:
                  wten, r0, c0, cn = gpieces[gl]
                  if gl >= 2:
                      sync.wait_ge(S["gm_dve"], gl - 1)
                  sync.dma_start(gbuf[:, gl % 2, 0:cn],
                                 wten.ap()[r0:r0 + 128, c0:c0 + cn]).then_inc(S["gm_ld%d" % (gl % 2)], 16)
                  gl += 1

          @blk.gpsimd
          def _(g):
              if P1_DBGXN:
                  for tt in range(NTT):
                      g.wait_ge(S["fm_act"], 2 * (tt + 1))
                      Qp = (tt // 8) % 2
                      tl = tt % 8
                      src_ap = xnfm[:, Qp, :].rearrange("p (kc tq) -> p kc tq", tq=1024)[:, :, tl * 128:tl * 128 + 128]
                      dst_ap = XN1DBG.ap()[:, :, tt * 128:(tt + 1) * 128].rearrange("kc p ml -> p kc ml")
                      g.dma_start(dst_ap, src_ap).then_inc(S["qst%d" % (tt % 2)], 16)
                  g.wait_ge(S["qst0"], 16 * (NTT // 2))
                  g.wait_ge(S["qst1"], 16 * (NTT // 2))
              if P1_MM:
                  ggl = 0
                  for E in range(96 * 4):
                      if P1_GAMMA and P1_GGP and E % 3 == 0 and ggl < NG:
                          wten, r0, c0, cn = gpieces[ggl]
                          if ggl >= 2:
                              g.wait_ge(S["gm_dve"], ggl - 1)
                          g.dma_start(gbuf[:, ggl % 2, 0:cn],
                                      wten.ap()[r0:r0 + 128, c0:c0 + cn]).then_inc(S["gm_ld%d" % (ggl % 2)], 16)
                          ggl += 1
                      g.wait_ge(S["ev_act"], E + 1)
                      g.dma_start(ev_dst(E), evbuf[:, E % 4, :]).then_inc(S["qst%d" % (E % 4)], 16)
                  g.wait_ge(S["qst0"], 16 * 96)
                  g.wait_ge(S["qst1"], 16 * 96)
                  g.wait_ge(S["qst2"], 16 * 96)
                  g.wait_ge(S["qst3"], 16 * 96)

          @blk.vector
          def _(v):
              gi = 0
              def edge(ins):
                  edge2(v, ins)
              for tt in range(NTT):
                  b = tt % 2
                  v.wait_ge(S["x_ld%d" % (tt % 2)], 16 * (tt // 2 + 1))
                  edge(v.reduce_sum(stat[:, b, 0:1], xbuf[:, b, :], AX.X))
                  edge(v.tensor_scalar(stat[:, b, 2:3], stat[:, b, 0:1], -1.0 / D, None, ALU.mult))
                  edge(v.scalar_tensor_tensor(stat[:, b, 3:4], stat[:, b, 0:1], -1.0 / D,
                                              stat[:, b, 0:1], ALU.mult, ALU.mult))
                  v.wait_ge(S["ln_act"], 2 * tt + 1)
                  v.tensor_add(stat[:, b, 4:5], stat[:, b, 1:2],
                               stat[:, b, 3:4]).then_inc(S["ln_dve"], 1)
                  v.wait_ge(S["ln_act"], 2 * tt + 2)
                  edge(v.reciprocal(stat[:, b, 6:7], stat[:, b, 5:6]))
                  if tt >= 8:
                      v.wait_ge(S["tr_pe"], tt - 7)
                  v.tensor_scalar(xntm[:, tt % 8, :], xbuf[:, b, :],
                                  stat[:, b, 2:3], stat[:, b, 6:7],
                                  ALU.add, ALU.mult).then_inc(S["ln_dve"], 1)
                  while P1_GAMMA and (not P1_GLATE) and gi < gsplit[tt + 1]:
                      v.wait_ge(S["gm_ld%d" % (gi % 2)], 16 * (gi // 2 + 1))
                      cn = gpieces[gi][3]
                      v.tensor_reduce(gcols[:, gi:gi + 1], gbuf[:, gi % 2, 0:cn], AX.X,
                                      ALU.add, apply_absolute_value=True
                                      ).then_inc(S["gm_dve"], 1)
                      v.wait_ge(S["gm_dve"], gi + 1)
                      gi += 1
              if P1_GAMMA and P1_GLATE:
                  while gi < NG:
                      v.wait_ge(S["gm_ld%d" % (gi % 2)], 16 * (gi // 2 + 1))
                      cn = gpieces[gi][3]
                      v.tensor_reduce(gcols[:, gi:gi + 1], gbuf[:, gi % 2, 0:cn], AX.X,
                                      ALU.add, apply_absolute_value=True
                                      ).then_inc(S["gm_dve"], 1)
                      gi += 1
              if P1_GAMMA and P1_GFIN:
                  edge(v.reduce_sum(gcols[:, 136:137], gcols[:, 0:GW1], AX.X))
                  edge(v.reduce_sum(gcols[:, 137:138], gcols[:, GW1:GW2], AX.X))
                  edge(v.reduce_sum(gcols[:, 138:139], gcols[:, GW2:NG], AX.X))
                  v.nop().then_inc(S["gm_dve"], 1)
                  v.wait_ge(S["gm_pe"], 1)
                  for wi in range(3):
                      edge(v.tensor_scalar(gthr[:, 2 * wi:2 * wi + 1], mm0[:, 480 + wi:481 + wi],
                                           0.5 / NE_FF, 0.5 * QEPS, ALU.mult, ALU.add))
                      edge(v.tensor_scalar(gthr[:, 2 * wi + 1:2 * wi + 2],
                                           gthr[:, 2 * wi:2 * wi + 1],
                                           -1.0, None, ALU.mult))
                  v.nop().then_inc(S["gm_dve"], 3)

          @blk.tensor
          def _(t):
              for tt in range(8):
                  t.wait_ge(S["ln_dve"], 2 * (tt + 1))
                  if tt >= 2:
                      t.wait_ge(S["fm_act"], 2 * tt - 2)
                  ins = None
                  for kc in range(NKD):
                      ins = t.transpose(ptb[tt % 2][:, kc * 128:(kc + 1) * 128],
                                        xntm[:, tt % 8, kc * 128:(kc + 1) * 128], identbf[:])
                  ins.then_inc(S["tr_pe"], 1)
              for Q in range(4):
                  if P1_MM:
                      t.wait_ge(S["fm_act"], 16 * (Q + 1))
                      for m in range(NMQKV):
                          gidx = Q * NMQKV + m
                          t.wait_ge(S["wt_ld%d" % (gidx % 2)], 16 * (gidx // 2 + 1))
                          for half in range(2):
                              E = 2 * gidx + half
                              if E >= 4:
                                  t.wait_ge(S["ev_act"], E - 3)
                              ins = None
                              for kc in range(NKD):
                                  ins = t.matmul(mmb[E % 4][:],
                                                 wtile[:, gidx % 2, kc * 128:(kc + 1) * 128],
                                                 xnfm[:, Q % 2, kc * 1024 + half * 512:
                                                      kc * 1024 + half * 512 + 512],
                                                 start=(kc == 0), stop=(kc == NKD - 1))
                              ins.then_inc(S["mm_pe"], 1)
                  if Q < 3:
                      for tt2 in range(8):
                          tt = (Q + 1) * 8 + tt2
                          t.wait_ge(S["ln_dve"], 2 * (tt + 1))
                          t.wait_ge(S["fm_act"], 2 * tt - 2)
                          ins = None
                          for kc in range(NKD):
                              ins = t.transpose(ptb[tt % 2][:, kc * 128:(kc + 1) * 128],
                                                xntm[:, tt % 8, kc * 128:(kc + 1) * 128],
                                                identbf[:])
                          ins.then_inc(S["tr_pe"], 1)
              if P1_GAMMA and P1_GFIN:
                  t.wait_ge(S["gm_dve"], NG + 1)
                  if P1_MM:
                      t.wait_ge(S["ev_act"], 96 * 4)
                  ins = None
                  for wi in range(3):
                      ins = t.matmul(mm0[:, 480 + wi:481 + wi], ones_f[:],
                                     gcols[:, 136 + wi:137 + wi], start=True, stop=True)
                  ins.then_inc(S["gm_pe"], 1)

          @blk.scalar
          def _(sc):
              def xload(tt):
                  if tt >= NTT:
                      return
                  if tt >= 2:
                      sc.wait_ge(S["ln_dve"], 2 * (tt - 1))
                  sc.dma_start(xbuf[:, tt % 2, :],
                               x_in.ap()[tt * 128:(tt + 1) * 128, :]).then_inc(S["x_ld%d" % (tt % 2)], 16)

              def stats(tt):
                  b = tt % 2
                  sc.wait_ge(S["x_ld%d" % (tt % 2)], 16 * (tt // 2 + 1))
                  sc.activation(sqbuf[:], xbuf[:, b, :], AF.Square,
                                accum_out=stat[:, b, 1:2]).then_inc(S["ln_act"], 1)
                  sc.wait_ge(S["ln_dve"], 2 * tt + 1)
                  sc.activation(stat[:, b, 5:6], stat[:, b, 4:5], AF.Sqrt,
                                bias=epscol[:, 0:1], scale=1.0 / D).then_inc(S["ln_act"], 1)
                  xload(tt + 2)

              def fmev(tt):
                  sc.wait_ge(S["tr_pe"], tt + 1)
                  if P1_DBGXN and tt >= 16:
                      sc.wait_ge(S["qst%d" % (tt % 2)], 16 * ((tt - 16) // 2 + 1))
                  if P1_MM and tt >= 16:
                      sc.wait_ge(S["mm_pe"], 2 * NMQKV * (tt // 8 - 1))
                  Qp = (tt // 8) % 2
                  tl = tt % 8
                  dst = xnfm[:, Qp, :].rearrange("p (kc tq) -> p kc tq", tq=1024)[:, :, tl * 128:tl * 128 + 128]
                  sc.copy(dst, ptb[tt % 2][:, :].rearrange("p (kc ml) -> p kc ml", ml=128)
                          ).then_inc(S["fm_act"], 2)

              xload(0)
              xload(1)
              for tt in range(8):
                  stats(tt)
              for tt in range(8):
                  fmev(tt)
              for Q in range(4):
                  if Q < 3:
                      for tt2 in range(8):
                          stats((Q + 1) * 8 + tt2)
                  if P1_MM:
                      for m in range(NMQKV):
                          for half in range(2):
                              E = (Q * NMQKV + m) * 2 + half
                              sc.wait_ge(S["mm_pe"], E + 1)
                              if E >= 4:
                                  sc.wait_ge(S["qst%d" % (E % 4)], 16 * (E // 4))
                              sc.activation(evbuf[:, E % 4, :], mmb[E % 4][:], AF.Identity,
                                            bias=bWcol[:, m:m + 1],
                                            scale=(SCALE if m < 16 else 1.0)
                                            ).then_inc(S["ev_act"], 1)
                  if Q < 3:
                      for tt2 in range(8):
                          fmev((Q + 1) * 8 + tt2)


    def phase2():
      ITS = 128
      kmax = [4 * (it % 8) + 4 for it in range(ITS)]
      cumS = [0]
      for it in range(ITS):
          cumS.append(cumS[-1] + kmax[it])

      p2_sems = ["kg0", "kg1", "kg2", "kg3", "vf0", "vf1", "vt_pe", "vt_act",
                 "sv_pe", "cv", "q0", "q1", "s_kt", "e_kt", "m_kt", "z_fin",
                 "rz", "nrm", "pv_fin", "ae", "ast0", "ast1"]
      for n in p2_sems:
          S[n] = nc.alloc_semaphore(n)

      with (
          nc.sbuf_tensor("p2_kg", [128, 4 * T], BF16) as kgsb,
          nc.sbuf_tensor("p2_vf", [128, 2, T], BF16) as vfm,
          nc.sbuf_tensor("p2_vt", [128, 4 * T], BF16) as vtsb,
          nc.sbuf_tensor("p2_eb", [128, 2, 32 * 512], BF16) as ebuf,
          nc.sbuf_tensor("p2_q", [128, 2, 512], BF16) as qbuf,
          nc.sbuf_tensor("p2_iz", [128, 2, 512], F32) as izb,
          nc.sbuf_tensor("p2_as", [128, 2, 512], BF16) as astg,
          nc.sbuf_tensor("p2_cc", [128, 4, 8], F32) as ccol,
          nc.sbuf_tensor("p2_rb", [128, 4480], F32) as rbig,
          nc.sbuf_tensor("p2_tb", [128, 4, 512], F32) as tbuf,
          nc.psum_tensor("p2_s0", [128, 512], F32) as ps_s0,
          nc.psum_tensor("p2_s1", [128, 512], F32) as ps_s1,
          nc.psum_tensor("p2_z", [128, 512], F32) as ps_z,
          nc.psum_tensor("p2_a0", [128, 512], F32) as ps_a0,
          nc.psum_tensor("p2_a1", [128, 512], F32) as ps_a1,
          nc.psum_tensor("p2_s2", [128, 512], F32) as ps_s2,
          nc.psum_tensor("p2_vt0", [128, 1024], BF16) as ps_vt0,
          nc.psum_tensor("p2_sf", [128, 8], F32) as ps_sf,
          nc.Block() as blk,
      ):
          psS = [ps_s0, ps_s1, ps_s2]
          psA = [ps_a0, ps_a1]
          psVt = [ps_vt0]

          @blk.sync
          def _(sync):
              sync.dma_start(rbig[:], c_rbig.ap()).then_inc(S["c_ld"], 16)
              for it in range(ITS):
                  head, qg = it // 8, it % 8
                  if it >= 2:
                      sync.wait_ge(S["s_kt"], cumS[it - 1])
                  sync.dma_start(qbuf[:, it % 2, :],
                                 QS.ap()[head][:, qg * 512:qg * 512 + 512]
                                 ).then_inc(S["q%d" % (it % 2)], 16)
                  if it >= 1:
                      sync.wait_ge(S["ae"], it)
                      pit = it - 1
                      ph, pq = pit // 8, pit % 8
                      sync.dma_start(ATT.ap()[ph][:, pq * 512:pq * 512 + 512],
                                     astg[:, pit % 2, :]).then_inc(S["ast%d" % (pit % 2)], 16)
              sync.wait_ge(S["ae"], ITS)
              sync.dma_start(ATT.ap()[15][:, 7 * 512:8 * 512],
                             astg[:, (ITS - 1) % 2, :]).then_inc(S["ast%d" % ((ITS - 1) % 2)], 16)
              sync.wait_ge(S["ast0"], 16 * (ITS // 2))
              sync.wait_ge(S["ast1"], 16 * (ITS // 2))

          @blk.gpsimd
          def _(g):
              for gr in range(4):
                  if gr >= 1:
                      g.wait_ge(S["s_kt"], cumS[32 * gr])
                  for hl in range(4):
                      g.dma_start(kgsb[:, hl * T:(hl + 1) * T],
                                  KS.ap()[gr * 4 + hl]).then_inc(S["kg%d" % hl], 16)
                  for hl in range(4):
                      head = gr * 4 + hl
                      if head >= 2:
                          g.wait_ge(S["vt_pe"], 4 * (head - 1))
                      g.dma_start(vfm[:, head % 2, :],
                                  VS.ap()[head]).then_inc(S["vf%d" % (head % 2)], 16)

          @blk.tensor
          def _(t):
              def pv(pit):
                  pl, pm = (pit // 8) % 4, kmax[pit]
                  t.wait_ge(S["nrm"], pit + 1)
                  t.wait_ge(S["vt_act"], 4 * (pit // 8) + 4)
                  if pit >= 2:
                      t.wait_ge(S["ae"], pit - 1)
                  ins = None
                  for kt in range(pm):
                      ins = t.matmul(psA[pit % 2][:],
                                     vtsb[:, pl * T + kt * 128:pl * T + (kt + 1) * 128],
                                     ebuf[:, pit % 2, kt * 512:(kt + 1) * 512],
                                     start=(kt == 0), stop=(kt == pm - 1))
                  ins.then_inc(S["pv_fin"], 1)

              for gr in range(4):
                  if gr >= 1:
                      pv(32 * gr - 1)
                  for hl in range(4):
                      head = gr * 4 + hl
                      t.wait_ge(S["vf%d" % (head % 2)], 16 * (head // 2 + 1))
                      for b in range(4):
                          gb = head * 4 + b
                          if gb >= 1:
                              t.wait_ge(S["vt_act"], gb)
                          ins = None
                          for i in range(8):
                              kt = b * 8 + i
                              ins = t.transpose(psVt[0][:, i * 128:(i + 1) * 128],
                                                vfm[:, head % 2, kt * 128:(kt + 1) * 128],
                                                identbf[:])
                          ins.then_inc(S["vt_pe"], 1)
                      t.wait_ge(S["vt_act"], 4 * head + 4)
                      if head >= 1:
                          t.wait_ge(S["cv"], head)
                      ins = None
                      for qg in range(7):
                          for kt in range(4 * qg + 4, 32):
                              ins = t.matmul(ps_sf[:, qg:qg + 1],
                                             vtsb[:, hl * T + kt * 128:hl * T + (kt + 1) * 128],
                                             ones_bf[:, 0:1],
                                             start=(kt == 4 * qg + 4), stop=(kt == 31))
                      ins.then_inc(S["sv_pe"], 1)
                  for hl in range(4):
                      for qg in range(8):
                          it = (gr * 4 + hl) * 8 + qg
                          km = kmax[it]
                          t.wait_ge(S["q%d" % (it % 2)], 16 * (it // 2 + 1))
                          t.wait_ge(S["kg%d" % hl], 16 * (gr + 1))

                          def S_mm(kt):
                              gs = cumS[it] + kt
                              if gs >= 3:
                                  t.wait_ge(S["fm_act"], 64 + gs - 2)
                              t.matmul(psS[gs % 3][:],
                                       kgsb[:, hl * T + kt * 128:hl * T + (kt + 1) * 128],
                                       qbuf[:, it % 2, :],
                                       start=True, stop=True).then_inc(S["s_kt"], 1)

                          def Z_mm(kt):
                              t.wait_ge(S["e_kt"], cumS[it] + kt + 1)
                              if kt >= 4 * qg:
                                  t.wait_ge(S["m_kt"], 4 * it + (kt - 4 * qg) + 1)
                              if kt == 0:
                                  t.wait_ge(S["rz"], it)
                              ins = t.matmul(ps_z[:], ones_bf[:],
                                             ebuf[:, it % 2, kt * 512:(kt + 1) * 512],
                                             start=(kt == 0), stop=(kt == km - 1))
                              if kt == km - 1:
                                  ins.then_inc(S["z_fin"], 1)

                          S_mm(0)
                          for kt in range(1, km):
                              S_mm(kt)
                              Z_mm(kt - 1)
                          Z_mm(km - 1)
                          if it % 32 >= 1:
                              pv(it - 1)
              pv(ITS - 1)

          @blk.scalar
          def _(sc):
              def att_evac(pit):
                  pq = pit % 8
                  sc.wait_ge(S["pv_fin"], pit + 1)
                  if pit >= 2:
                      sc.wait_ge(S["ast%d" % (pit % 2)], 16 * ((pit - 2) // 2 + 1))
                  if pq == 7:
                      sc.copy(astg[:, pit % 2, :], psA[pit % 2][:]).then_inc(S["ae"], 1)
                  else:
                      sc.activation(astg[:, pit % 2, :], psA[pit % 2][:],
                                    AF.Identity,
                                    bias=ccol[:, (pit // 8) % 4, pq:pq + 1]
                                    ).then_inc(S["ae"], 1)

              for gr in range(4):
                  if gr >= 1:
                      att_evac(32 * gr - 1)
                  for hl in range(4):
                      head = gr * 4 + hl
                      for b in range(4):
                          gb = head * 4 + b
                          sc.wait_ge(S["vt_pe"], gb + 1)
                          sc.copy(vtsb[:, hl * T + b * 1024:hl * T + (b + 1) * 1024],
                                  psVt[0][:, :]).then_inc(S["vt_act"], 1)
                      sc.wait_ge(S["sv_pe"], head + 1)
                      if head >= 4:
                          sc.wait_ge(S["ae"], 8 * (head - 3))
                      sc.activation(ccol[:, hl, 0:7], ps_sf[:, 0:7], AF.Identity,
                                    scale=CLAMP).then_inc(S["cv"], 1)
                  for hl in range(4):
                      for qg in range(8):
                          it = (gr * 4 + hl) * 8 + qg
                          km = kmax[it]
                          for kt in range(km):
                              gs = cumS[it] + kt
                              sc.wait_ge(S["fm_act"], 64 + gs + 1)
                              if kt == 0 and it >= 2:
                                  sc.wait_ge(S["pv_fin"], it - 1)
                              r = kt - 4 * qg
                              c0 = max(r, 0) * 128
                              sc.activation(
                                  ebuf[:, it % 2, kt * 512 + c0:(kt + 1) * 512],
                                  tbuf[:, gs % 4, c0:512], AF.Exp).then_inc(S["e_kt"], 1)
                          if it % 32 >= 1:
                              att_evac(it - 1)
              att_evac(ITS - 1)

          @blk.vector
          def _(v):
              v.wait_ge(S["c_ld"], 16 * 8)
              for it in range(ITS):
                  qg = it % 8
                  km = kmax[it]
                  hd = it // 8
                  def do_mask(r):
                      kt2 = 4 * qg + r
                      gs2 = cumS[it] + kt2
                      v.wait_ge(S["e_kt"], gs2 + 1)
                      if r > 0:
                          edge2(v, v.memset(ebuf[:, it % 2, kt2 * 512:kt2 * 512 + r * 128], 0.0))
                      v.tensor_tensor(ebuf[:, it % 2, kt2 * 512 + r * 128:kt2 * 512 + (r + 1) * 128],
                                      ebuf[:, it % 2, kt2 * 512 + r * 128:kt2 * 512 + (r + 1) * 128],
                                      diagkeep[:], ALU.mult).then_inc(S["m_kt"], 1)
                      v.wait_ge(S["m_kt"], 4 * it + r + 1)

                  for kt in range(km):
                      gs = cumS[it] + kt
                      v.wait_ge(S["s_kt"], gs + 1)
                      if gs >= 4:
                          v.wait_ge(S["e_kt"], gs - 3)
                      cc0 = 384 - 128 * kt + 512 * qg
                      v.scalar_tensor_tensor(tbuf[:, gs % 4, :], rbig[:, cc0:cc0 + 512],
                                             slopes_bc[:, hd:hd + 1], psS[gs % 3][:],
                                             ALU.mult, ALU.add).then_inc(S["fm_act"], 1)
                      if kt - 1 - 4 * qg >= 0:
                          do_mask(kt - 1 - 4 * qg)
                  do_mask(3)
                  v.wait_ge(S["z_fin"], it + 1)
                  v.reciprocal(izb[:, it % 2, :], ps_z[:]).then_inc(S["rz"], 1)
                  v.wait_ge(S["rz"], it + 1)
                  ins = None
                  for kt in range(km):
                      ins = v.tensor_tensor(ebuf[:, it % 2, kt * 512:(kt + 1) * 512],
                                            ebuf[:, it % 2, kt * 512:(kt + 1) * 512],
                                            izb[:, it % 2, :], ALU.mult)
                  edge2(v, ins)
                  v.tensor_scalar(ebuf[:, it % 2, 0:km * 512],
                                  ebuf[:, it % 2, 0:km * 512],
                                  CLAMP, None, ALU.max).then_inc(S["nrm"], 1)



    def phase3():
      p3_sems = ["ow_ld", "brow_ld", "ab0", "ab1", "x3_0", "x3_1", "p3mm",
                 "p3ev", "p3tr", "p3o1", "p3sqa", "p3lna", "p3v1", "p3x2",
                 "p3v2", "p3xn", "p3xf", "p3xs0", "p3xs1", "p3ns0", "p3ns1"]
      for n in p3_sems:
          S[n] = nc.alloc_semaphore(n)

      with (
          nc.sbuf_tensor("p3_ow", [128, NKD * NKD * 128], BF16) as owt,
          nc.sbuf_tensor("p3_ab", [128, 2, NKD * 128], BF16) as attb,
          nc.sbuf_tensor("p3_of", [128, 2, D], BF16) as of1,
          nc.sbuf_tensor("p3_o1tm", [128, 2, D], BF16) as o1tm,
          nc.sbuf_tensor("p3_x", [128, 2, D], F32) as xb3,
          nc.sbuf_tensor("p3_x2", [128, 2, D], F32) as x2b,
          nc.sbuf_tensor("p3_xn2", [128, 2, D], BF16) as xn2tm,
          nc.sbuf_tensor("p3_xns", [128, 2, D], BF16) as xn2st,
          nc.sbuf_tensor("p3_st", [128, 2, 16], F32) as st3,
          nc.sbuf_tensor("p3_sq", [128, D], F32) as sq3,
          nc.sbuf_tensor("p3_brow", [1, D], F32) as brow,
          nc.psum_tensor("p3_b0", [128, 512], F32) as p3b0,
          nc.psum_tensor("p3_b1", [128, 512], F32) as p3b1,
          nc.psum_tensor("p3_b2", [128, 512], F32) as p3b2,
          nc.psum_tensor("p3_b3", [128, 512], F32) as p3b3,
          nc.psum_tensor("p3_t0", [128, 2048], BF16) as p3t0,
          nc.psum_tensor("p3_t1", [128, 2048], BF16) as p3t1,
          nc.Block() as blk,
      ):
          psB = [p3b0, p3b1, p3b2, p3b3]
          psT = [p3t0, p3t1]

          @blk.gpsimd
          def _(g):
              for m in range(NKD):
                  g.dma_start(owt[:, m * 2048:(m + 1) * 2048],
                              OWT3.ap()[m].rearrange("p kc ml -> p (kc ml)")
                              ).then_inc(S["ow_ld"], 16)
              g.dma_start(brow[0:1, :],
                          owb.ap().rearrange("(o n) -> o n", o=1)).then_inc(S["brow_ld"], 16)
              for ts in range(NTT):
                  if ts >= 2:
                      g.wait_ge(S["p3mm"], NKD * (ts - 1))
                  g.dma_start(attb[:, ts % 2, :].rearrange("p (c n) -> p c n", n=128),
                              ATT.ap()[:, :, ts * 128:(ts + 1) * 128]
                              .rearrange("c p n -> p c n")).then_inc(S["ab%d" % (ts % 2)], 16)

          @blk.sync
          def _(sync):
              for ts in range(NTT):
                  if ts >= 2:
                      sync.wait_ge(S["p3x2"], ts - 1)
                  sync.dma_start(xb3[:, ts % 2, :],
                                 x_in.ap()[ts * 128:(ts + 1) * 128, :]
                                 ).then_inc(S["x3_%d" % (ts % 2)], 16)
                  sync.wait_ge(S["p3x2"], ts + 1)
                  sync.dma_start(X2S.ap()[ts * 128:(ts + 1) * 128, :],
                                 x2b[:, ts % 2, :]).then_inc(S["p3xs%d" % (ts % 2)], 16)
                  sync.wait_ge(S["p3xf"], ts + 1)
                  sync.dma_start(XN2S.ap()[:, :, ts * 128:(ts + 1) * 128]
                                 .rearrange("c p n -> p c n"),
                                 xn2st[:, ts % 2, :].rearrange("p (c n) -> p c n", n=128)
                                 ).then_inc(S["p3ns%d" % (ts % 2)], 16)
              sync.wait_ge(S["p3xs0"], 16 * (NTT // 2))
              sync.wait_ge(S["p3xs1"], 16 * (NTT // 2))
              sync.wait_ge(S["p3ns0"], 16 * (NTT // 2))
              sync.wait_ge(S["p3ns1"], 16 * (NTT // 2))

          @blk.tensor
          def _(t):
              t.wait_ge(S["brow_ld"], 16)
              t.wait_ge(S["ow_ld"], 16 * NKD)
              for ts in range(NTT):
                  t.wait_ge(S["ab%d" % (ts % 2)], 16 * (ts // 2 + 1))
                  for m in range(NKD):
                      bk, q = m // 4, m % 4
                      if q == 0 and ts >= 1:
                          t.wait_ge(S["p3ev"], 4 * ts + bk - 3)
                      bank = psB[bk]
                      t.matmul(bank[:, q * 128:(q + 1) * 128],
                               brow[0:1, m * 128:(m + 1) * 128],
                               onesrow[0:1, 0:128], start=True, stop=False)
                      ins = None
                      for kc in range(NKD):
                          ins = t.matmul(bank[:, q * 128:(q + 1) * 128],
                                         owt[:, m * 2048 + kc * 128:m * 2048 + (kc + 1) * 128],
                                         attb[:, ts % 2, kc * 128:(kc + 1) * 128],
                                         start=False, stop=(kc == NKD - 1))
                      ins.then_inc(S["p3mm"], 1)
                  t.wait_ge(S["p3ev"], 4 * ts + 4)
                  if ts >= 2:
                      t.wait_ge(S["p3xf"], ts - 1)
                  ins = None
                  for m in range(NKD):
                      ins = t.transpose(psT[ts % 2][:, m * 128:(m + 1) * 128],
                                        of1[:, ts % 2, m * 128:(m + 1) * 128], identbf[:])
                  ins.then_inc(S["p3tr"], 1)
                  if ts >= 1:
                      pt = ts - 1
                      t.wait_ge(S["p3xn"], pt + 1)
                      t.wait_ge(S["p3o1"], pt + 1)
                      ins = None
                      for m in range(NKD):
                          ins = t.transpose(psT[pt % 2][:, m * 128:(m + 1) * 128],
                                            xn2tm[:, pt % 2, m * 128:(m + 1) * 128], identbf[:])
                      ins.then_inc(S["p3tr"], 1)
              pt = NTT - 1
              t.wait_ge(S["p3xn"], pt + 1)
              t.wait_ge(S["p3o1"], pt + 1)
              ins = None
              for m in range(NKD):
                  ins = t.transpose(psT[pt % 2][:, m * 128:(m + 1) * 128],
                                    xn2tm[:, pt % 2, m * 128:(m + 1) * 128], identbf[:])
              ins.then_inc(S["p3tr"], 1)

          def o1cnt(ts):
              return 1 if ts == 0 else 2 * ts

          def xncnt(k):
              return 2 * k + 3 if k < NTT - 1 else 2 * NTT

          @blk.scalar
          def _(sc):
              def xn2_evac(pt):
                  sc.wait_ge(S["p3tr"], xncnt(pt))
                  if pt >= 2:
                      sc.wait_ge(S["p3ns%d" % (pt % 2)], 16 * ((pt - 2) // 2 + 1))
                  sc.copy(xn2st[:, pt % 2, :], psT[pt % 2][:, :]).then_inc(S["p3xf"], 1)

              for ts in range(NTT):
                  b = ts % 2
                  for bk in range(4):
                      sc.wait_ge(S["p3mm"], ts * NKD + 4 * (bk + 1))
                      if ts >= 2:
                          sc.wait_ge(S["p3tr"], o1cnt(ts - 2))
                      sc.copy(of1[:, b, bk * 512:(bk + 1) * 512],
                              psB[bk][:, 0:512]).then_inc(S["p3ev"], 1)
                  sc.wait_ge(S["p3tr"], o1cnt(ts))
                  if ts >= 2:
                      sc.wait_ge(S["p3x2"], ts - 1)
                  sc.copy(o1tm[:, b, :], psT[ts % 2][:, :]).then_inc(S["p3o1"], 1)
                  if ts >= 1:
                      xn2_evac(ts - 1)
                  sc.wait_ge(S["p3o1"], ts + 1)
                  sc.activation(sq3[:], o1tm[:, b, :], AF.Square,
                                accum_out=st3[:, b, 10:11]).then_inc(S["p3sqa"], 1)
                  sc.wait_ge(S["p3v1"], ts + 1)
                  sc.activation(st3[:, b, 8:9], st3[:, b, 2:3], AF.Sqrt,
                                bias=epscol[:, 0:1], scale=1.0 / D).then_inc(S["p3lna"], 1)
                  sc.wait_ge(S["p3x2"], ts + 1)
                  sc.activation(sq3[:], x2b[:, b, :], AF.Square,
                                accum_out=st3[:, b, 12:13]).then_inc(S["p3sqa"], 1)
                  sc.wait_ge(S["p3v2"], ts + 1)
                  sc.activation(st3[:, b, 9:10], st3[:, b, 6:7], AF.Sqrt,
                                bias=epscol[:, 0:1], scale=1.0 / D).then_inc(S["p3lna"], 1)
              xn2_evac(NTT - 1)

          @blk.vector
          def _(v):
              for ts in range(NTT):
                  b = ts % 2
                  v.wait_ge(S["p3o1"], ts + 1)
                  edge2(v, v.reduce_sum(st3[:, b, 0:1], o1tm[:, b, :], AX.X))
                  edge2(v, v.tensor_scalar(st3[:, b, 1:2], st3[:, b, 0:1], -1.0 / D,
                                           None, ALU.mult))
                  edge2(v, v.scalar_tensor_tensor(st3[:, b, 3:4], st3[:, b, 0:1], -1.0 / D,
                                                  st3[:, b, 0:1], ALU.mult, ALU.mult))
                  v.wait_ge(S["p3sqa"], 2 * ts + 1)
                  v.tensor_add(st3[:, b, 2:3], st3[:, b, 10:11],
                               st3[:, b, 3:4]).then_inc(S["p3v1"], 1)
                  v.wait_ge(S["p3lna"], 2 * ts + 1)
                  edge2(v, v.reciprocal(st3[:, b, 4:5], st3[:, b, 8:9]))
                  v.wait_ge(S["x3_%d" % b], 16 * (ts // 2 + 1))
                  if ts >= 2:
                      v.wait_ge(S["p3xs%d" % b], 16 * ((ts - 2) // 2 + 1))
                  edge2(v, v.tensor_scalar(x2b[:, b, :], o1tm[:, b, :],
                                           st3[:, b, 1:2], st3[:, b, 4:5],
                                           ALU.add, ALU.mult))
                  v.tensor_add(x2b[:, b, :], x2b[:, b, :],
                               xb3[:, b, :]).then_inc(S["p3x2"], 1)
                  v.wait_ge(S["p3x2"], ts + 1)
                  edge2(v, v.reduce_sum(st3[:, b, 5:6], x2b[:, b, :], AX.X))
                  edge2(v, v.tensor_scalar(st3[:, b, 7:8], st3[:, b, 5:6], -1.0 / D,
                                           None, ALU.mult))
                  edge2(v, v.scalar_tensor_tensor(st3[:, b, 11:12], st3[:, b, 5:6], -1.0 / D,
                                                  st3[:, b, 5:6], ALU.mult, ALU.mult))
                  v.wait_ge(S["p3sqa"], 2 * ts + 2)
                  v.tensor_add(st3[:, b, 6:7], st3[:, b, 12:13],
                               st3[:, b, 11:12]).then_inc(S["p3v2"], 1)
                  v.wait_ge(S["p3lna"], 2 * ts + 2)
                  edge2(v, v.reciprocal(st3[:, b, 13:14], st3[:, b, 9:10]))
                  if ts >= 2:
                      v.wait_ge(S["p3tr"], xncnt(ts - 2))
                  v.tensor_scalar(xn2tm[:, b, :], x2b[:, b, :],
                                  st3[:, b, 7:8], st3[:, b, 13:14],
                                  ALU.add, ALU.mult).then_inc(S["p3xn"], 1)


    def phase3b():
      strips = []
      for rc in range(NFF):
          strips.append((0, rc, D, w1, W1T3))
      for rc in range(NFF):
          strips.append((1, rc, D, w2, W2T3))
      for rc in range(NKD):
          strips.append((2, rc, FF, w3, W3T3))
      NSTR = len(strips)
      items = []
      for i, (wsel, rc, cn, _, _) in enumerate(strips):
          ntile = cn // 128
          nit = (ntile + 7) // 8
          for il in range(nit):
              items.append((i, il, min(8, ntile - il * 8)))
      NIT = len(items)
      first_item = {}
      last_item = {}
      for gidx, (i, il, nt) in enumerate(items):
          if i not in first_item:
              first_item[i] = gidx
          last_item[i] = gidx

      p3b_sems = ["qw0", "qw1", "qd", "qp", "qa", "qs0", "qs1"]
      for n in p3b_sems:
          S[n] = nc.alloc_semaphore(n)

      with (
          nc.sbuf_tensor("q_w", [128, 2, FF], F32) as qwb,
          nc.sbuf_tensor("q_neg", [128, FF], F32) as qneg,
          nc.sbuf_tensor("q_out", [128, 2, FF], BF16) as qout,
          nc.sbuf_tensor("q_stage", [128, 2, FF], BF16) as qstage,
          nc.psum_tensor("q_t0", [128, 1024], BF16) as qt0,
          nc.psum_tensor("q_t1", [128, 1024], BF16) as qt1,
          nc.Block() as blk,
      ):
          psQ = [qt0, qt1]

          @blk.gpsimd
          def _(g):
              for i in range(NSTR + 2):
                  if i < NSTR:
                      wsel, rc, cn, srct, dstt = strips[i]
                      if i >= 2:
                          g.wait_ge(S["qd"], i - 1)
                      g.dma_start(qwb[:, i % 2, 0:cn],
                                  srct.ap()[rc * 128:(rc + 1) * 128, :]
                                  ).then_inc(S["qw%d" % (i % 2)], 16)
                  if i >= 2:
                      j = i - 2
                      wsel, rc, cn, srct, dstt = strips[j]
                      g.wait_ge(S["qa"], last_item[j] + 1)
                      g.dma_start(dstt.ap()[rc].rearrange("p kc ml -> p (kc ml)"),
                                  qstage[:, j % 2, 0:cn]).then_inc(S["qs%d" % (j % 2)], 16)
              g.wait_ge(S["qs0"], 16 * ((NSTR + 1) // 2))
              g.wait_ge(S["qs1"], 16 * (NSTR // 2))

          @blk.vector
          def _(v):
              for i, (wsel, rc, cn, srct, dstt) in enumerate(strips):
                  v.wait_ge(S["qw%d" % (i % 2)], 16 * (i // 2 + 1))
                  if i >= 2:
                      v.wait_ge(S["qp"], last_item[i - 2] + 1)
                  edge2(v, v.tensor_scalar(qneg[:, 0:cn], qwb[:, i % 2, 0:cn],
                                           gthr[:, 2 * wsel + 1:2 * wsel + 2], None,
                                           ALU.is_lt))
                  v.scalar_tensor_tensor(qout[:, i % 2, 0:cn], qwb[:, i % 2, 0:cn],
                                         gthr[:, 2 * wsel:2 * wsel + 1], qneg[:, 0:cn],
                                         ALU.is_gt, ALU.subtract).then_inc(S["qd"], 1)
                  v.wait_ge(S["qd"], i + 1)

          @blk.tensor
          def _(t):
              for gidx, (i, il, nt) in enumerate(items):
                  t.wait_ge(S["qd"], i + 1)
                  if gidx >= 2:
                      t.wait_ge(S["qa"], gidx - 1)
                  ins = None
                  for j in range(nt):
                      kt = il * 8 + j
                      ins = t.transpose(psQ[gidx % 2][:, j * 128:(j + 1) * 128],
                                        qout[:, i % 2, kt * 128:(kt + 1) * 128], identbf[:])
                  ins.then_inc(S["qp"], 1)

          @blk.scalar
          def _(sc):
              for gidx, (i, il, nt) in enumerate(items):
                  sc.wait_ge(S["qp"], gidx + 1)
                  if i >= 2 and il == 0:
                      sc.wait_ge(S["qs%d" % (i % 2)], 16 * ((i - 2) // 2 + 1))
                  sc.copy(qstage[:, i % 2, il * 1024:il * 1024 + nt * 128],
                          psQ[gidx % 2][:, 0:nt * 128]).then_inc(S["qa"], 1)

    def phase4():
      NSL = 8
      p4_sems = ["xnl0", "xnl1", "w12_0", "w12_1", "w3l0", "w3l1", "x2l0", "x2l1",
                 "f1mm", "f2mm", "f3mm", "sgk", "hidk", "foev", "p4tr", "o3ev",
                 "p4sqa", "p4lna", "p4v", "p4y", "yst0", "yst1"]
      for n in p4_sems:
          S[n] = nc.alloc_semaphore(n)

      with (
          nc.sbuf_tensor("p4_xn", [128, 2, NKD * 512], BF16) as xnsl,
          nc.sbuf_tensor("p4_w12", [128, 2, 4 * NKD * 128], BF16) as wst,
          nc.sbuf_tensor("p4_w3", [128, 2, NFF * 128], BF16) as w3st,
          nc.sbuf_tensor("p4_sg", [128, NFF * 512], BF16) as sg,
          nc.sbuf_tensor("p4_fo", [128, 1, NKD * 512], BF16) as fo,
          nc.sbuf_tensor("p4_x2", [128, 2, D], F32) as x2r,
          nc.sbuf_tensor("p4_o3", [128, 2, D], BF16) as o3tm,
          nc.sbuf_tensor("p4_y", [128, 2, D], F32) as ybuf,
          nc.sbuf_tensor("p4_st", [128, 2, 8], F32) as st4,
          nc.sbuf_tensor("p4_sq", [128, D], F32) as sq4,
          nc.psum_tensor("p4_f0", [128, 512], F32) as p4f0,
          nc.psum_tensor("p4_f1", [128, 512], F32) as p4f1,
          nc.psum_tensor("p4_f2", [128, 512], F32) as p4f2,
          nc.psum_tensor("p4_f3", [128, 512], F32) as p4f3,
          nc.psum_tensor("p4_t0", [128, 2048], BF16) as p4t0,
          nc.psum_tensor("p4_t1", [128, 2048], BF16) as p4t1,
          nc.Block() as blk,
      ):
          psF = [p4f0, p4f1, p4f2, p4f3]
          psT4 = [p4t0, p4t1]
          NFG = 11

          @blk.gpsimd
          def _(g):
              for ts in range(NSL):
                  if ts >= 2:
                      g.wait_ge(S["f2mm"], NFF * (ts - 1))
                  g.dma_start(xnsl[:, ts % 2, :].rearrange("p (c n) -> p c n", n=512),
                              XN2S.ap()[:, :, ts * 512:(ts + 1) * 512]
                              .rearrange("c p n -> p c n")).then_inc(S["xnl%d" % (ts % 2)], 16)
                  for wi in range(2):
                      base = ts * 2 * NFG + wi * NFG
                      for fg in range(NFG):
                          gl = base + fg
                          if gl >= 2:
                              pts, prem = (gl - 2) // (2 * NFG), (gl - 2) % (2 * NFG)
                              pwi, pfg = prem // NFG, prem % NFG
                              g.wait_ge(S["f1mm" if pwi == 0 else "f2mm"],
                                        pts * NFF + 4 * pfg + 4)
                          src = (W1T3 if wi == 0 else W2T3)
                          g.dma_start(wst[:, gl % 2, :]
                                      .rearrange("p (f kc ml) -> p f kc ml", f=4, kc=NKD),
                                      src.ap()[4 * fg:4 * fg + 4]
                                      .rearrange("f p kc ml -> p f kc ml")
                                      ).then_inc(S["w12_%d" % (gl % 2)], 16)
                  for d in range(NKD):
                      gl3 = ts * NKD + d
                      if gl3 >= 2:
                          g.wait_ge(S["f3mm"], gl3 - 1)
                      g.dma_start(w3st[:, gl3 % 2, :],
                                  W3T3.ap()[d].rearrange("p kc ml -> p (kc ml)")
                                  ).then_inc(S["w3l%d" % (gl3 % 2)], 16)

          @blk.sync
          def _(sync):
              for ts in range(NSL):
                  for u in range(4):
                      tu = ts * 4 + u
                      if tu >= 2:
                          sync.wait_ge(S["p4y"], tu - 1)
                      sync.dma_start(x2r[:, tu % 2, :],
                                     X2S.ap()[tu * 128:(tu + 1) * 128, :]
                                     ).then_inc(S["x2l%d" % (tu % 2)], 16)
                      sync.wait_ge(S["p4y"], tu + 1)
                      sync.dma_start(y_out.ap()[tu * 128:(tu + 1) * 128, :],
                                     ybuf[:, tu % 2, :]).then_inc(S["yst%d" % (tu % 2)], 16)
              sync.wait_ge(S["yst0"], 16 * (NTT // 2))
              sync.wait_ge(S["yst1"], 16 * (NTT // 2))

          @blk.tensor
          def _(t):
              for ts in range(NSL):
                  t.wait_ge(S["xnl%d" % (ts % 2)], 16 * (ts // 2 + 1))
                  for fg in range(NFG):
                      gl = ts * 2 * NFG + fg
                      t.wait_ge(S["w12_%d" % (gl % 2)], 16 * (gl // 2 + 1))
                      for fl in range(4):
                          f = 4 * fg + fl
                          gf = ts * NFF + f
                          if ts >= 1 and f == 0:
                              t.wait_ge(S["foev"], ts * NKD)
                          if gf >= 4:
                              t.wait_ge(S["sgk"], gf - 3)
                          ins = None
                          for kc in range(NKD):
                              ins = t.matmul(psF[f % 4][:],
                                             wst[:, gl % 2, (fl * NKD + kc) * 128:(fl * NKD + kc + 1) * 128],
                                             xnsl[:, ts % 2, kc * 512:(kc + 1) * 512],
                                             start=(kc == 0), stop=(kc == NKD - 1))
                          ins.then_inc(S["f1mm"], 1)
                  for fg in range(NFG):
                      gl = ts * 2 * NFG + NFG + fg
                      t.wait_ge(S["w12_%d" % (gl % 2)], 16 * (gl // 2 + 1))
                      for fl in range(4):
                          f = 4 * fg + fl
                          if f == 0:
                              t.wait_ge(S["sgk"], (ts + 1) * NFF)
                          if f >= 4:
                              t.wait_ge(S["hidk"], ts * NFF + f - 3)
                          ins = None
                          for kc in range(NKD):
                              ins = t.matmul(psF[f % 4][:],
                                             wst[:, gl % 2, (fl * NKD + kc) * 128:(fl * NKD + kc + 1) * 128],
                                             xnsl[:, ts % 2, kc * 512:(kc + 1) * 512],
                                             start=(kc == 0), stop=(kc == NKD - 1))
                          ins.then_inc(S["f2mm"], 1)
                  for d in range(NKD):
                      gl3 = ts * NKD + d
                      t.wait_ge(S["w3l%d" % (gl3 % 2)], 16 * (gl3 // 2 + 1))
                      if d == 0:
                          t.wait_ge(S["hidk"], (ts + 1) * NFF)
                      if ts * NKD + d >= 4:
                          t.wait_ge(S["foev"], ts * NKD + d - 3)
                      ins = None
                      for fc in range(NFF):
                          ins = t.matmul(psF[d % 4][:],
                                         w3st[:, gl3 % 2, fc * 128:(fc + 1) * 128],
                                         sg[:, fc * 512:(fc + 1) * 512],
                                         start=(fc == 0), stop=(fc == NFF - 1))
                      ins.then_inc(S["f3mm"], 1)
                  for u in range(4):
                      tu = ts * 4 + u
                      t.wait_ge(S["foev"], ts * NKD + NKD)
                      if tu >= 2:
                          t.wait_ge(S["o3ev"], tu - 1)
                      ins = None
                      for c in range(NKD):
                          ins = t.transpose(psT4[tu % 2][:, c * 128:(c + 1) * 128],
                                            fo[:, 0, c * 512 + u * 128:c * 512 + u * 128 + 128],
                                            identbf[:])
                      ins.then_inc(S["p4tr"], 1)

          @blk.scalar
          def _(sc):
              for ts in range(NSL):
                  for f in range(NFF):
                      sc.wait_ge(S["f1mm"], ts * NFF + f + 1)
                      sc.activation(sg[:, f * 512:(f + 1) * 512], psF[f % 4][:],
                                    AF.Sigmoid).then_inc(S["sgk"], 1)
                  for d in range(NKD):
                      sc.wait_ge(S["f3mm"], ts * NKD + d + 1)
                      if ts >= 1 and d == 0:
                          sc.wait_ge(S["p4tr"], 4 * ts)
                      sc.copy(fo[:, 0, d * 512:(d + 1) * 512],
                              psF[d % 4][:]).then_inc(S["foev"], 1)
                  for u in range(4):
                      tu = ts * 4 + u
                      sc.wait_ge(S["p4tr"], tu + 1)
                      if tu >= 2:
                          sc.wait_ge(S["p4y"], tu - 1)
                      sc.copy(o3tm[:, tu % 2, :], psT4[tu % 2][:, :]).then_inc(S["o3ev"], 1)
                      sc.wait_ge(S["o3ev"], tu + 1)
                      sc.activation(sq4[:], o3tm[:, tu % 2, :], AF.Square,
                                    accum_out=st4[:, tu % 2, 0:1]).then_inc(S["p4sqa"], 1)
                      sc.wait_ge(S["p4v"], tu + 1)
                      sc.activation(st4[:, tu % 2, 4:5], st4[:, tu % 2, 3:4], AF.Sqrt,
                                    bias=epscol[:, 0:1], scale=1.0 / D).then_inc(S["p4lna"], 1)

          @blk.vector
          def _(v):
              for ts in range(NSL):
                  for f in range(NFF):
                      v.wait_ge(S["f2mm"], ts * NFF + f + 1)
                      v.wait_ge(S["sgk"], ts * NFF + f + 1)
                      v.tensor_tensor(sg[:, f * 512:(f + 1) * 512], psF[f % 4][:],
                                      sg[:, f * 512:(f + 1) * 512],
                                      ALU.mult).then_inc(S["hidk"], 1)
                  for u in range(4):
                      tu = ts * 4 + u
                      b = tu % 2
                      v.wait_ge(S["o3ev"], tu + 1)
                      edge2(v, v.reduce_sum(st4[:, b, 1:2], o3tm[:, b, :], AX.X))
                      edge2(v, v.tensor_scalar(st4[:, b, 2:3], st4[:, b, 1:2], -1.0 / D,
                                               None, ALU.mult))
                      edge2(v, v.scalar_tensor_tensor(st4[:, b, 5:6], st4[:, b, 1:2],
                                                      -1.0 / D, st4[:, b, 1:2],
                                                      ALU.mult, ALU.mult))
                      v.wait_ge(S["p4sqa"], tu + 1)
                      v.tensor_add(st4[:, b, 3:4], st4[:, b, 0:1],
                                   st4[:, b, 5:6]).then_inc(S["p4v"], 1)
                      v.wait_ge(S["p4lna"], tu + 1)
                      edge2(v, v.reciprocal(st4[:, b, 6:7], st4[:, b, 4:5]))
                      v.wait_ge(S["x2l%d" % b], 16 * (tu // 2 + 1))
                      if tu >= 2:
                          v.wait_ge(S["yst%d" % b], 16 * ((tu - 2) // 2 + 1))
                      edge2(v, v.tensor_scalar(ybuf[:, b, :], o3tm[:, b, :],
                                               st4[:, b, 2:3], st4[:, b, 6:7],
                                               ALU.add, ALU.mult))
                      v.tensor_add(ybuf[:, b, :], ybuf[:, b, :],
                                   x2r[:, b, :]).then_inc(S["p4y"], 1)

    if 0 in phases:
        phase0()
    if 1 in phases:
        phase1()
    if 2 in phases:
        phase2()
    if 3 in phases:
        phase3()
    if 35 in phases:
        phase3b()
    if 4 in phases:
        phase4()

    if debug_taps:
        dbg_sem = nc.alloc_semaphore("dbg_sem")
        with nc.Block() as blk:
            outs = []
            for name in debug_taps:
                src = scratch[name]
                dt = nc.dram_tensor("dbg_" + name, list(src.shape), src.dtype,
                                    kind="ExternalOutput")
                outs.append((dt, src))

            @blk.gpsimd
            def _(g):
                n = 0
                for dt, src in outs:
                    g.dma_start(dt.ap(), src.ap()).then_inc(dbg_sem, 16)
                    n += 1
                g.wait_ge(dbg_sem, 16 * n)

    return nc


def host_consts():
    ii = np.arange(128)
    dbig = (ii[:, None] + 128.0 * (np.arange(32)[None, :] - 32.0)).astype(np.float32)
    diagkeep = (ii[:, None] <= ii[None, :]).astype(BF)
    return {
        "c_identbf": np.eye(128, dtype=np.float32).astype(BF),
        "c_rbig": (np.arange(128)[:, None] + 384.0 - np.arange(4480)[None, :]).astype(np.float32),
        "c_identf": np.eye(128, dtype=np.float32),
        "c_dbig": dbig,
        "c_diagkeep": diagkeep,
    }


def make_in_maps(L0_attn_qkv, L0_attn_in_proj, L0_attn_in_proj_bias, L0_attn_out_w,
                 L0_attn_out_b, L0_attn_alibi_slopes, L0_ffn_w1, L0_ffn_w2, L0_ffn_w3):
    consts = host_consts()
    in_maps = []
    for b in range(8):
        m = {
            "x": np.ascontiguousarray(np.asarray(L0_attn_qkv)[b]),
            "ipw": np.asarray(L0_attn_in_proj),
            "ipb": np.asarray(L0_attn_in_proj_bias),
            "oww": np.asarray(L0_attn_out_w),
            "owb": np.asarray(L0_attn_out_b),
            "alibi": np.asarray(L0_attn_alibi_slopes),
            "w1": np.asarray(L0_ffn_w1),
            "w2": np.asarray(L0_ffn_w2),
            "w3": np.asarray(L0_ffn_w3),
        }
        m.update(consts)
        in_maps.append(m)
    return in_maps


def kernel(**inputs):
    nc = build(DEBUG_TAPS)
    in_maps = make_in_maps(**inputs)
    res = run_bass_kernel_spmd(nc, in_maps, core_ids=list(range(8)))
    return np.stack([r["y"] for r in res.results], axis=0)


# version 04 (final)
# speedup vs baseline: 1.1756x; 1.1756x over previous
import numpy as np
import ml_dtypes
import concourse.bass as bass
import concourse.mybir as mybir
from concourse.bass_utils import run_bass_kernel_spmd

F32 = mybir.dt.float32
BF16 = mybir.dt.bfloat16
AF = mybir.ActivationFunctionType
ALU = mybir.AluOpType
AX = mybir.AxisListType

D = 2048
T = 4096
H = 16
DH = 128
FF = 5632
NKD = D // 128
NMQKV = 3 * D // 128
NFF = FF // 128
NTT = T // 128
SCALE = float(1.0 / np.sqrt(DH))
EPS = 1e-5
QEPS = 1e-8
CLAMP = 1e-10
NE_FF = FF * D

BF = ml_dtypes.bfloat16
DEBUG_TAPS = ()
P1_MM = True
P1_GAMMA = True
P1_DBGXN = False
P1_GFIN = True
P1_GLATE = True
P1_GGP = False


def build(debug_taps=(), phases=(0, 1, 2, 3, 35, 4)):
    nc = bass.Bass("TRN2", target_bir_lowering=False)

    x_in = nc.dram_tensor("x", [T, D], F32, kind="ExternalInput")
    ipw = nc.dram_tensor("ipw", [3 * D, D], F32, kind="ExternalInput")
    ipb = nc.dram_tensor("ipb", [3 * D], F32, kind="ExternalInput")
    oww = nc.dram_tensor("oww", [D, D], F32, kind="ExternalInput")
    owb = nc.dram_tensor("owb", [D], F32, kind="ExternalInput")
    alibi = nc.dram_tensor("alibi", [H], F32, kind="ExternalInput")
    w1 = nc.dram_tensor("w1", [FF, D], F32, kind="ExternalInput")
    w2 = nc.dram_tensor("w2", [FF, D], F32, kind="ExternalInput")
    w3 = nc.dram_tensor("w3", [D, FF], F32, kind="ExternalInput")
    c_identbf = nc.dram_tensor("c_identbf", [128, 128], BF16, kind="ExternalInput")
    c_identf = nc.dram_tensor("c_identf", [128, 128], F32, kind="ExternalInput")
    c_dbig = nc.dram_tensor("c_dbig", [128, 32], F32, kind="ExternalInput")
    c_diagkeep = nc.dram_tensor("c_diagkeep", [128, 128], BF16, kind="ExternalInput")

    y_out = nc.dram_tensor("y", [T, D], F32, kind="ExternalOutput")

    IPT3 = nc.dram_tensor("IPT3", [NMQKV, 128, NKD, 128], BF16, kind="Internal")
    OWT3 = nc.dram_tensor("OWT3", [NKD, 128, NKD, 128], BF16, kind="Internal")
    W1T3 = nc.dram_tensor("W1T3", [NFF, 128, NKD, 128], BF16, kind="Internal")
    W2T3 = nc.dram_tensor("W2T3", [NFF, 128, NKD, 128], BF16, kind="Internal")
    W3T3 = nc.dram_tensor("W3T3", [NKD, 128, NFF, 128], BF16, kind="Internal")
    IPBF = nc.dram_tensor("IPBF", [3 * D, D], BF16, kind="Internal")
    OWBF = nc.dram_tensor("OWBF", [D, D], BF16, kind="Internal")
    QS = nc.dram_tensor("QS", [H, 128, T], BF16, kind="Internal")
    KS = nc.dram_tensor("KS", [H, 128, T], BF16, kind="Internal")
    VS = nc.dram_tensor("VS", [H, 128, T], BF16, kind="Internal")
    ATT = nc.dram_tensor("ATT", [H, 128, T], BF16, kind="Internal")
    X2S = nc.dram_tensor("X2S", [T, D], F32, kind="Internal")
    XN2S = nc.dram_tensor("XN2S", [NKD, 128, T], BF16, kind="Internal")
    XN1DBG = nc.dram_tensor("XN1DBG", [NKD, 128, T], BF16, kind="Internal")
    scratch = {"QS": QS, "KS": KS, "VS": VS, "ATT": ATT, "X2S": X2S, "XN2S": XN2S,
               "IPT3": IPT3, "OWT3": OWT3, "W1T3": W1T3, "W2T3": W2T3, "W3T3": W3T3,
               "IPBF": IPBF, "XN1DBG": XN1DBG}

    identbf = nc.alloc_sbuf_tensor("identbf", [128, 128], BF16)
    identf = nc.alloc_sbuf_tensor("identf", [128, 128], F32)
    diagkeep = nc.alloc_sbuf_tensor("diagkeep", [128, 128], BF16)
    ones_bf = nc.alloc_sbuf_tensor("ones_bf", [128, 128], BF16)
    ones_f = nc.alloc_sbuf_tensor("ones_f", [128, 128], F32)
    epscol = nc.alloc_sbuf_tensor("epscol", [128, 1], F32)
    bWcol = nc.alloc_sbuf_tensor("bWcol", [128, 64], F32)
    slopes_bc = nc.alloc_sbuf_tensor("slopes_bc", [128, H], F32)
    jtab = nc.alloc_sbuf_tensor("jtab", [128, H * 32], F32)
    gthr = nc.alloc_sbuf_tensor("gthr", [128, 6], F32)
    onesrow = nc.alloc_sbuf_tensor("onesrow", [1, 128], F32)

    sem_names = [
        "c_ld", "p0_ldi", "p0_ldo", "w_ld0", "w_ld1", "p0_pe", "p0_act",
        "p0_dve", "p0_st0", "p0_st1",
        "x_ld0", "x_ld1", "ln_act", "ln_dve", "tr_pe", "fm_act",
        "wt_ld0", "wt_ld1", "mm_pe", "ev_act",
        "qst0", "qst1", "qst2", "qst3", "gm_ld0", "gm_ld1",
        "gm_dve", "gm_pe", "ch",
    ]
    S = {n: nc.alloc_semaphore(n) for n in sem_names}
    _ch = {"n": 0}

    def edge2(v, ins):
        _ch["n"] += 1
        ins.then_inc(S["ch"], 1)
        v.wait_ge(S["ch"], _ch["n"])

    NW = NMQKV + NKD

    def phase0():
        with (
          nc.sbuf_tensor("p0_btile", [128, 128], F32) as btile,
          nc.sbuf_tensor("p0_arow", [1, 128], F32) as arow,
          nc.sbuf_tensor("p0_onesf1", [1, 128], F32) as onesf1,
          nc.sbuf_tensor("p0_wstrip", [128, 2, D], BF16) as wstrip,
          nc.sbuf_tensor("p0_ostrip", [128, 2, D], BF16) as ostrip,
          nc.sbuf_tensor("p0_dbig", [128, 32], F32) as dbig,
          nc.psum_tensor("p0_ps", [128, 512], F32) as p0ps,
          nc.psum_tensor("p0_pst0", [128, 2048], BF16) as p0pst0,
          nc.psum_tensor("p0_pst1", [128, 2048], BF16) as p0pst1,
          nc.Block() as blk,
      ):
          p0pst = [p0pst0, p0pst1]

          @blk.gpsimd
          def _(g):
              for i in range(6):
                  g.dma_start(IPBF.ap()[i * 1024:(i + 1) * 1024, :],
                              ipw.ap()[i * 1024:(i + 1) * 1024, :]).then_inc(S["p0_ldi"], 16)
              for i in range(2):
                  g.dma_start(OWBF.ap()[i * 1024:(i + 1) * 1024, :],
                              oww.ap()[i * 1024:(i + 1) * 1024, :]).then_inc(S["p0_ldo"], 16)
              for j in range(NW):
                  if j < NMQKV:
                      src = IPBF.ap()[j * 128:(j + 1) * 128, :]
                      g.wait_ge(S["p0_ldi"], 16 * 6)
                  else:
                      src = OWBF.ap()[(j - NMQKV) * 128:(j - NMQKV + 1) * 128, :]
                      g.wait_ge(S["p0_ldo"], 16 * 2)
                  if j >= 2:
                      g.wait_ge(S["p0_pe"], j)
                  g.dma_start(wstrip[:, j % 2, :], src).then_inc(S["w_ld%d" % (j % 2)], 16)

          @blk.sync
          def _(sync):
              sync.dma_start(identbf[:], c_identbf.ap()).then_inc(S["c_ld"], 16)
              sync.dma_start(identf[:], c_identf.ap()).then_inc(S["c_ld"], 16)
              sync.dma_start(diagkeep[:], c_diagkeep.ap()).then_inc(S["c_ld"], 16)
              sync.dma_start(dbig[:], c_dbig.ap()).then_inc(S["c_ld"], 16)
              sync.dma_start(btile[0:48, :],
                             ipb.ap().rearrange("(c p) -> c p", p=128)).then_inc(S["c_ld"], 16)
              sync.dma_start(btile[48:64, :],
                             owb.ap().rearrange("(c p) -> c p", p=128)).then_inc(S["c_ld"], 16)
              sync.dma_start(arow[0:1, 0:H],
                             alibi.ap().rearrange("(o h) -> o h", o=1)).then_inc(S["c_ld"], 16)
              for j in range(NW):
                  sync.wait_ge(S["p0_act"], 2 + j)
                  dst = IPT3.ap()[j] if j < NMQKV else OWT3.ap()[j - NMQKV]
                  sync.dma_start(dst.rearrange("p kc ml -> p (kc ml)"),
                                 ostrip[:, j % 2, :]).then_inc(S["p0_st%d" % (j % 2)], 16)
              sync.wait_ge(S["p0_st0"], 16 * (NW // 2))
              sync.wait_ge(S["p0_st1"], 16 * (NW // 2))

          @blk.vector
          def _(v):
              v.memset(ones_bf[:], 1.0)
              v.memset(ones_f[:], 1.0)
              v.memset(epscol[:], EPS)
              v.memset(onesf1[:], 1.0)
              v.memset(onesrow[:], 1.0)
              v.wait_ge(S["c_ld"], 16 * 7)
              v.memset(btile[64:128, :], 0.0).then_inc(S["p0_dve"], 1)
              v.wait_ge(S["p0_act"], 1)
              for h in range(H):
                  v.tensor_scalar(jtab[:, h * 32:(h + 1) * 32], dbig[:],
                                  slopes_bc[:, h:h + 1], None, ALU.mult).then_inc(S["p0_dve"], 1)

          @blk.tensor
          def _(t):
              t.wait_ge(S["c_ld"], 16 * 7)
              t.wait_ge(S["p0_dve"], 1)
              t.transpose(p0ps[:, 0:128], btile[:], identf[:])
              t.matmul(p0ps[:, 128:128 + H], onesf1[0:1, :], arow[0:1, 0:H],
                       start=True, stop=True).then_inc(S["p0_pe"], 1)
              for j in range(NW):
                  t.wait_ge(S["w_ld%d" % (j % 2)], 16 * (j // 2 + 1))
                  if j >= 2:
                      t.wait_ge(S["p0_act"], j)
                  ins = None
                  for kc in range(NKD):
                      ins = t.transpose(p0pst[j % 2][:, kc * 128:(kc + 1) * 128],
                                        wstrip[:, j % 2, kc * 128:(kc + 1) * 128], identbf[:])
                  ins.then_inc(S["p0_pe"], 1)

          @blk.scalar
          def _(sc):
              sc.wait_ge(S["p0_pe"], 1)
              sc.mul(bWcol[:, 0:16], p0ps[:, 0:16], SCALE)
              sc.copy(bWcol[:, 16:64], p0ps[:, 16:64])
              sc.copy(slopes_bc[:], p0ps[:, 128:128 + H]).then_inc(S["p0_act"], 1)
              for j in range(NW):
                  sc.wait_ge(S["p0_pe"], 2 + j)
                  if j >= 2:
                      sc.wait_ge(S["p0_st%d" % (j % 2)], 16 * (j // 2))
                  sc.copy(ostrip[:, j % 2, :], p0pst[j % 2][:, :]).then_inc(S["p0_act"], 1)

    def phase1():
      gpieces = []
      for wten in (w1, w2):
          for r in range(FF // 128):
              gpieces.append((wten, r * 128, 0, D))
      for r in range(D // 128):
          for c0, cn in ((0, 2048), (2048, 2048), (4096, 1536)):
              gpieces.append((w3, r * 128, c0, cn))
      NG = len(gpieces)
      GW1, GW2 = NFF, 2 * NFF
      gsplit = [int(np.ceil(NG * k / NTT)) for k in range(NTT + 1)]
      gl_per_q = [(NG * (q + 1) + 3) // 4 - (NG * q + 3) // 4 for q in range(4)]

      with (
          nc.sbuf_tensor("p1_x", [128, 2, D], F32) as xbuf,
          nc.sbuf_tensor("p1_sq", [128, D], F32) as sqbuf,
          nc.sbuf_tensor("p1_st", [128, 2, 8], F32) as stat,
          nc.sbuf_tensor("p1_xn", [128, 8, D], BF16) as xntm,
          nc.sbuf_tensor("p1_fm", [128, 2, NKD * 1024], BF16) as xnfm,
          nc.sbuf_tensor("p1_wt", [128, 2, NKD * 128], BF16) as wtile,
          nc.sbuf_tensor("p1_ev", [128, 4, 512], BF16) as evbuf,
          nc.sbuf_tensor("p1_gb", [128, 4, D], F32) as gbuf,
          nc.sbuf_tensor("p1_gc", [128, 140], F32) as gcols,
          nc.psum_tensor("p1_mm0", [128, 512], F32) as mm0,
          nc.psum_tensor("p1_mm1", [128, 512], F32) as mm1,
          nc.psum_tensor("p1_mm2", [128, 512], F32) as mm2,
          nc.psum_tensor("p1_mm3", [128, 512], F32) as mm3,
          nc.psum_tensor("p1_pt0", [128, 2048], BF16) as pt0,
          nc.psum_tensor("p1_pt1", [128, 2048], BF16) as pt1,
          nc.Block() as blk,
      ):
          mmb = [mm0, mm1, mm2, mm3]
          ptb = [pt0, pt1]

          def ev_dst(E):
              m = (E // 2) % NMQKV
              Q = E // 96
              half = E % 2
              h = m % 16
              ten = QS if m < 16 else (KS if m < 32 else VS)
              off = Q * 1024 + half * 512
              return ten.ap()[h][:, off:off + 512]

          @blk.sync
          def _(sync):
              gl = 0
              for Q in range(4):
                  for m in range(NMQKV):
                      gidx = Q * NMQKV + m
                      if P1_MM:
                          if gidx >= 2:
                              sync.wait_ge(S["mm_pe"], 2 * (gidx - 1))
                          sync.dma_start(wtile[:, gidx % 2, :],
                                         IPT3.ap()[m].rearrange("p kc ml -> p (kc ml)")
                                         ).then_inc(S["wt_ld%d" % (gidx % 2)], 16)
                      while P1_GAMMA and gl < NG and gl < (gidx + 1) * NG // 192 + 1:
                          wten, r0, c0, cn = gpieces[gl]
                          if gl >= 2:
                              sync.wait_ge(S["gm_dve"], gl - 1)
                          sync.dma_start(gbuf[:, gl % 4, 0:cn],
                                         wten.ap()[r0:r0 + 128, c0:c0 + cn]
                                         ).then_inc(S["gm_ld%d" % (gl % 2)], 16)
                          gl += 1


          @blk.gpsimd
          def _(g):
              if P1_DBGXN:
                  for tt in range(NTT):
                      g.wait_ge(S["fm_act"], 2 * (tt + 1))
                      Qp = (tt // 8) % 2
                      tl = tt % 8
                      src_ap = xnfm[:, Qp, :].rearrange("p (kc tq) -> p kc tq", tq=1024)[:, :, tl * 128:tl * 128 + 128]
                      dst_ap = XN1DBG.ap()[:, :, tt * 128:(tt + 1) * 128].rearrange("kc p ml -> p kc ml")
                      g.dma_start(dst_ap, src_ap).then_inc(S["qst%d" % (tt % 2)], 16)
                  g.wait_ge(S["qst0"], 16 * (NTT // 2))
                  g.wait_ge(S["qst1"], 16 * (NTT // 2))
              if P1_MM:
                  for E in range(96 * 4):
                      g.wait_ge(S["ev_act"], E + 1)
                      g.dma_start(ev_dst(E), evbuf[:, E % 4, :]).then_inc(S["qst%d" % (E % 4)], 16)
                  g.wait_ge(S["qst0"], 16 * 96)
                  g.wait_ge(S["qst1"], 16 * 96)
                  g.wait_ge(S["qst2"], 16 * 96)
                  g.wait_ge(S["qst3"], 16 * 96)

          @blk.vector
          def _(v):
              gi = 0
              def edge(ins):
                  edge2(v, ins)
              for tt in range(NTT):
                  b = tt % 2
                  v.wait_ge(S["x_ld%d" % (tt % 2)], 16 * (tt // 2 + 1))
                  edge(v.reduce_sum(stat[:, b, 0:1], xbuf[:, b, :], AX.X))
                  edge(v.tensor_scalar(stat[:, b, 2:3], stat[:, b, 0:1], -1.0 / D, None, ALU.mult))
                  edge(v.scalar_tensor_tensor(stat[:, b, 3:4], stat[:, b, 0:1], -1.0 / D,
                                              stat[:, b, 0:1], ALU.mult, ALU.mult))
                  v.wait_ge(S["ln_act"], 2 * tt + 1)
                  v.tensor_add(stat[:, b, 4:5], stat[:, b, 1:2],
                               stat[:, b, 3:4]).then_inc(S["ln_dve"], 1)
                  v.wait_ge(S["ln_act"], 2 * tt + 2)
                  edge(v.reciprocal(stat[:, b, 6:7], stat[:, b, 5:6]))
                  if tt >= 8:
                      v.wait_ge(S["tr_pe"], tt - 7)
                  v.tensor_scalar(xntm[:, tt % 8, :], xbuf[:, b, :],
                                  stat[:, b, 2:3], stat[:, b, 6:7],
                                  ALU.add, ALU.mult).then_inc(S["ln_dve"], 1)

              if P1_GAMMA and P1_GFIN:
                  v.wait_ge(S["gm_dve"], NG)
                  edge(v.reduce_sum(gcols[:, 136:137], gcols[:, 0:GW1], AX.X))
                  edge(v.reduce_sum(gcols[:, 137:138], gcols[:, GW1:GW2], AX.X))
                  edge(v.reduce_sum(gcols[:, 138:139], gcols[:, GW2:NG], AX.X))
                  v.nop().then_inc(S["gm_dve"], 1)
                  v.wait_ge(S["gm_pe"], 1)
                  for wi in range(3):
                      edge(v.tensor_scalar(gthr[:, 2 * wi:2 * wi + 1], mm0[:, 480 + wi:481 + wi],
                                           0.5 / NE_FF, 0.5 * QEPS, ALU.mult, ALU.add))
                      edge(v.tensor_scalar(gthr[:, 2 * wi + 1:2 * wi + 2],
                                           gthr[:, 2 * wi:2 * wi + 1],
                                           -1.0, None, ALU.mult))
                  v.nop().then_inc(S["gm_dve"], 3)

          @blk.tensor
          def _(t):
              for tt in range(8):
                  t.wait_ge(S["ln_dve"], 2 * (tt + 1))
                  if tt >= 2:
                      t.wait_ge(S["fm_act"], 2 * tt - 2)
                  ins = None
                  for kc in range(NKD):
                      ins = t.transpose(ptb[tt % 2][:, kc * 128:(kc + 1) * 128],
                                        xntm[:, tt % 8, kc * 128:(kc + 1) * 128], identbf[:])
                  ins.then_inc(S["tr_pe"], 1)
              for Q in range(4):
                  if P1_MM:
                      t.wait_ge(S["fm_act"], 16 * (Q + 1))
                      for m in range(NMQKV):
                          gidx = Q * NMQKV + m
                          t.wait_ge(S["wt_ld%d" % (gidx % 2)], 16 * (gidx // 2 + 1))
                          for half in range(2):
                              E = 2 * gidx + half
                              if E >= 4:
                                  t.wait_ge(S["ev_act"], E - 3)
                              ins = None
                              for kc in range(NKD):
                                  ins = t.matmul(mmb[E % 4][:],
                                                 wtile[:, gidx % 2, kc * 128:(kc + 1) * 128],
                                                 xnfm[:, Q % 2, kc * 1024 + half * 512:
                                                      kc * 1024 + half * 512 + 512],
                                                 start=(kc == 0), stop=(kc == NKD - 1))
                              ins.then_inc(S["mm_pe"], 1)
                  if Q < 3:
                      for tt2 in range(8):
                          tt = (Q + 1) * 8 + tt2
                          t.wait_ge(S["ln_dve"], 2 * (tt + 1))
                          t.wait_ge(S["fm_act"], 2 * tt - 2)
                          ins = None
                          for kc in range(NKD):
                              ins = t.transpose(ptb[tt % 2][:, kc * 128:(kc + 1) * 128],
                                                xntm[:, tt % 8, kc * 128:(kc + 1) * 128],
                                                identbf[:])
                          ins.then_inc(S["tr_pe"], 1)
              if P1_GAMMA and P1_GFIN:
                  t.wait_ge(S["gm_dve"], NG + 1)
                  if P1_MM:
                      t.wait_ge(S["ev_act"], 96 * 4)
                  ins = None
                  for wi in range(3):
                      ins = t.matmul(mm0[:, 480 + wi:481 + wi], ones_f[:],
                                     gcols[:, 136 + wi:137 + wi], start=True, stop=True)
                  ins.then_inc(S["gm_pe"], 1)

          @blk.scalar
          def _(sc):
              gag = [0]

              def xload(tt):
                  if tt >= NTT:
                      return
                  if tt >= 2:
                      sc.wait_ge(S["ln_dve"], 2 * (tt - 1))
                  sc.dma_start(xbuf[:, tt % 2, :],
                               x_in.ap()[tt * 128:(tt + 1) * 128, :]).then_inc(S["x_ld%d" % (tt % 2)], 16)

              def stats(tt):
                  b = tt % 2
                  sc.wait_ge(S["x_ld%d" % (tt % 2)], 16 * (tt // 2 + 1))
                  sc.activation(sqbuf[:], xbuf[:, b, :], AF.Square,
                                accum_out=stat[:, b, 1:2]).then_inc(S["ln_act"], 1)
                  sc.wait_ge(S["ln_dve"], 2 * tt + 1)
                  sc.activation(stat[:, b, 5:6], stat[:, b, 4:5], AF.Sqrt,
                                bias=epscol[:, 0:1], scale=1.0 / D).then_inc(S["ln_act"], 1)
                  xload(tt + 2)

              def fmev(tt):
                  sc.wait_ge(S["tr_pe"], tt + 1)
                  if P1_DBGXN and tt >= 16:
                      sc.wait_ge(S["qst%d" % (tt % 2)], 16 * ((tt - 16) // 2 + 1))
                  if P1_MM and tt >= 16:
                      sc.wait_ge(S["mm_pe"], 2 * NMQKV * (tt // 8 - 1))
                  Qp = (tt // 8) % 2
                  tl = tt % 8
                  dst = xnfm[:, Qp, :].rearrange("p (kc tq) -> p kc tq", tq=1024)[:, :, tl * 128:tl * 128 + 128]
                  sc.copy(dst, ptb[tt % 2][:, :].rearrange("p (kc ml) -> p kc ml", ml=128)
                          ).then_inc(S["fm_act"], 2)

              xload(0)
              xload(1)
              for tt in range(8):
                  stats(tt)
              for tt in range(8):
                  fmev(tt)
              for Q in range(4):
                  if Q < 3:
                      for tt2 in range(8):
                          stats((Q + 1) * 8 + tt2)
                  if P1_MM:
                      for m in range(NMQKV):
                          for half in range(2):
                              E = (Q * NMQKV + m) * 2 + half
                              while P1_GAMMA and gag[0] < NG and gag[0] < E * NG // 384:
                                  i = gag[0]
                                  wten, r0, c0, cn = gpieces[i]
                                  sc.wait_ge(S["gm_ld%d" % (i % 2)], 16 * (i // 2 + 1))
                                  sc.activation(sqbuf[:, 0:cn], gbuf[:, i % 4, 0:cn], AF.Abs,
                                                accum_out=gcols[:, i:i + 1]
                                                ).then_inc(S["gm_dve"], 1)
                                  sc.wait_ge(S["gm_dve"], i + 1)
                                  gag[0] += 1
                              sc.wait_ge(S["mm_pe"], E + 1)
                              if E >= 4:
                                  sc.wait_ge(S["qst%d" % (E % 4)], 16 * (E // 4))
                              sc.activation(evbuf[:, E % 4, :], mmb[E % 4][:], AF.Identity,
                                            bias=bWcol[:, m:m + 1],
                                            scale=(SCALE if m < 16 else 1.0)
                                            ).then_inc(S["ev_act"], 1)
                  if Q < 3:
                      for tt2 in range(8):
                          fmev((Q + 1) * 8 + tt2)
              while P1_GAMMA and gag[0] < NG:
                  i = gag[0]
                  wten, r0, c0, cn = gpieces[i]
                  sc.wait_ge(S["gm_ld%d" % (i % 2)], 16 * (i // 2 + 1))
                  sc.activation(sqbuf[:, 0:cn], gbuf[:, i % 4, 0:cn], AF.Abs,
                                accum_out=gcols[:, i:i + 1]).then_inc(S["gm_dve"], 1)
                  sc.wait_ge(S["gm_dve"], i + 1)
                  gag[0] += 1


    def phase2():
      ITS = 128
      kmax = [4 * (it % 8) + 4 for it in range(ITS)]
      cumS = [0]
      for it in range(ITS):
          cumS.append(cumS[-1] + kmax[it])

      p2_sems = ["kg0", "kg1", "kg2", "kg3", "vf0", "vf1", "vt_pe", "vt_act",
                 "sv_pe", "cv", "q0", "q1", "s_kt", "e_kt", "m_kt", "z_fin",
                 "rz", "nrm", "pv_fin", "ae", "ast0", "ast1"]
      for n in p2_sems:
          S[n] = nc.alloc_semaphore(n)

      with (
          nc.sbuf_tensor("p2_kg", [128, 4 * T], BF16) as kgsb,
          nc.sbuf_tensor("p2_vf", [128, 2, T], BF16) as vfm,
          nc.sbuf_tensor("p2_vt", [128, 4 * T], BF16) as vtsb,
          nc.sbuf_tensor("p2_eb", [128, 2, 32 * 512], BF16) as ebuf,
          nc.sbuf_tensor("p2_q", [128, 2, 512], BF16) as qbuf,
          nc.sbuf_tensor("p2_iz", [128, 2, 512], F32) as izb,
          nc.sbuf_tensor("p2_as", [128, 2, 512], BF16) as astg,
          nc.sbuf_tensor("p2_cc", [128, 4, 8], F32) as ccol,
          nc.psum_tensor("p2_s0", [128, 512], F32) as ps_s0,
          nc.psum_tensor("p2_s1", [128, 512], F32) as ps_s1,
          nc.psum_tensor("p2_z", [128, 512], F32) as ps_z,
          nc.psum_tensor("p2_a0", [128, 512], F32) as ps_a0,
          nc.psum_tensor("p2_a1", [128, 512], F32) as ps_a1,
          nc.psum_tensor("p2_vt0", [128, 1024], BF16) as ps_vt0,
          nc.psum_tensor("p2_vt1", [128, 1024], BF16) as ps_vt1,
          nc.psum_tensor("p2_sf", [128, 8], F32) as ps_sf,
          nc.Block() as blk,
      ):
          psS = [ps_s0, ps_s1]
          psA = [ps_a0, ps_a1]
          psVt = [ps_vt0, ps_vt1]

          @blk.sync
          def _(sync):
              for it in range(ITS):
                  head, qg = it // 8, it % 8
                  if it >= 2:
                      sync.wait_ge(S["s_kt"], cumS[it - 1])
                  sync.dma_start(qbuf[:, it % 2, :],
                                 QS.ap()[head][:, qg * 512:qg * 512 + 512]
                                 ).then_inc(S["q%d" % (it % 2)], 16)
                  if it >= 1:
                      sync.wait_ge(S["ae"], it)
                      pit = it - 1
                      ph, pq = pit // 8, pit % 8
                      sync.dma_start(ATT.ap()[ph][:, pq * 512:pq * 512 + 512],
                                     astg[:, pit % 2, :]).then_inc(S["ast%d" % (pit % 2)], 16)
              sync.wait_ge(S["ae"], ITS)
              sync.dma_start(ATT.ap()[15][:, 7 * 512:8 * 512],
                             astg[:, (ITS - 1) % 2, :]).then_inc(S["ast%d" % ((ITS - 1) % 2)], 16)
              sync.wait_ge(S["ast0"], 16 * (ITS // 2))
              sync.wait_ge(S["ast1"], 16 * (ITS // 2))

          @blk.gpsimd
          def _(g):
              for gr in range(4):
                  if gr >= 1:
                      g.wait_ge(S["s_kt"], cumS[32 * gr])
                  for hl in range(4):
                      g.dma_start(kgsb[:, hl * T:(hl + 1) * T],
                                  KS.ap()[gr * 4 + hl]).then_inc(S["kg%d" % hl], 16)
                  for hl in range(4):
                      head = gr * 4 + hl
                      if head >= 2:
                          g.wait_ge(S["vt_pe"], 4 * (head - 1))
                      g.dma_start(vfm[:, head % 2, :],
                                  VS.ap()[head]).then_inc(S["vf%d" % (head % 2)], 16)

          @blk.tensor
          def _(t):
              def pv(pit):
                  pl, pm = (pit // 8) % 4, kmax[pit]
                  t.wait_ge(S["nrm"], pit + 1)
                  t.wait_ge(S["vt_act"], 4 * (pit // 8) + 4)
                  if pit >= 2:
                      t.wait_ge(S["ae"], pit - 1)
                  ins = None
                  for kt in range(pm):
                      ins = t.matmul(psA[pit % 2][:],
                                     vtsb[:, pl * T + kt * 128:pl * T + (kt + 1) * 128],
                                     ebuf[:, pit % 2, kt * 512:(kt + 1) * 512],
                                     start=(kt == 0), stop=(kt == pm - 1))
                  ins.then_inc(S["pv_fin"], 1)

              for gr in range(4):
                  if gr >= 1:
                      pv(32 * gr - 1)
                  for hl in range(4):
                      head = gr * 4 + hl
                      t.wait_ge(S["vf%d" % (head % 2)], 16 * (head // 2 + 1))
                      for b in range(4):
                          gb = head * 4 + b
                          if gb >= 2:
                              t.wait_ge(S["vt_act"], gb - 1)
                          ins = None
                          for i in range(8):
                              kt = b * 8 + i
                              ins = t.transpose(psVt[gb % 2][:, i * 128:(i + 1) * 128],
                                                vfm[:, head % 2, kt * 128:(kt + 1) * 128],
                                                identbf[:])
                          ins.then_inc(S["vt_pe"], 1)
                      t.wait_ge(S["vt_act"], 4 * head + 4)
                      if head >= 1:
                          t.wait_ge(S["cv"], head)
                      ins = None
                      for qg in range(7):
                          for kt in range(4 * qg + 4, 32):
                              ins = t.matmul(ps_sf[:, qg:qg + 1],
                                             vtsb[:, hl * T + kt * 128:hl * T + (kt + 1) * 128],
                                             ones_bf[:, 0:1],
                                             start=(kt == 4 * qg + 4), stop=(kt == 31))
                      ins.then_inc(S["sv_pe"], 1)
                  for hl in range(4):
                      for qg in range(8):
                          it = (gr * 4 + hl) * 8 + qg
                          km = kmax[it]
                          t.wait_ge(S["q%d" % (it % 2)], 16 * (it // 2 + 1))
                          t.wait_ge(S["kg%d" % hl], 16 * (gr + 1))

                          def S_mm(kt):
                              gs = cumS[it] + kt
                              if gs >= 2:
                                  t.wait_ge(S["e_kt"], gs - 1)
                              t.matmul(psS[gs % 2][:],
                                       kgsb[:, hl * T + kt * 128:hl * T + (kt + 1) * 128],
                                       qbuf[:, it % 2, :],
                                       start=True, stop=True).then_inc(S["s_kt"], 1)

                          def Z_mm(kt):
                              t.wait_ge(S["e_kt"], cumS[it] + kt + 1)
                              if kt >= 4 * qg:
                                  t.wait_ge(S["m_kt"], 4 * it + (kt - 4 * qg) + 1)
                              if kt == 0:
                                  t.wait_ge(S["rz"], it)
                              ins = t.matmul(ps_z[:], ones_bf[:],
                                             ebuf[:, it % 2, kt * 512:(kt + 1) * 512],
                                             start=(kt == 0), stop=(kt == km - 1))
                              if kt == km - 1:
                                  ins.then_inc(S["z_fin"], 1)

                          S_mm(0)
                          for kt in range(1, km):
                              S_mm(kt)
                              Z_mm(kt - 1)
                          Z_mm(km - 1)
                          if it % 32 >= 1:
                              pv(it - 1)
              pv(ITS - 1)

          @blk.scalar
          def _(sc):
              def att_evac(pit):
                  pq = pit % 8
                  sc.wait_ge(S["pv_fin"], pit + 1)
                  if pit >= 2:
                      sc.wait_ge(S["ast%d" % (pit % 2)], 16 * ((pit - 2) // 2 + 1))
                  if pq == 7:
                      sc.copy(astg[:, pit % 2, :], psA[pit % 2][:]).then_inc(S["ae"], 1)
                  else:
                      sc.activation(astg[:, pit % 2, :], psA[pit % 2][:],
                                    AF.Identity,
                                    bias=ccol[:, (pit // 8) % 4, pq:pq + 1]
                                    ).then_inc(S["ae"], 1)

              for gr in range(4):
                  if gr >= 1:
                      att_evac(32 * gr - 1)
                  for hl in range(4):
                      head = gr * 4 + hl
                      for b in range(4):
                          gb = head * 4 + b
                          sc.wait_ge(S["vt_pe"], gb + 1)
                          sc.copy(vtsb[:, hl * T + b * 1024:hl * T + (b + 1) * 1024],
                                  psVt[gb % 2][:, :]).then_inc(S["vt_act"], 1)
                      sc.wait_ge(S["sv_pe"], head + 1)
                      if head >= 4:
                          sc.wait_ge(S["ae"], 8 * (head - 3))
                      sc.activation(ccol[:, hl, 0:7], ps_sf[:, 0:7], AF.Identity,
                                    scale=CLAMP).then_inc(S["cv"], 1)
                  for hl in range(4):
                      for qg in range(8):
                          it = (gr * 4 + hl) * 8 + qg
                          km = kmax[it]
                          for kt in range(km):
                              gs = cumS[it] + kt
                              sc.wait_ge(S["s_kt"], gs + 1)
                              if kt == 0 and it >= 2:
                                  sc.wait_ge(S["pv_fin"], it - 1)
                              ins = None
                              for ql in range(4):
                                  qsub = qg * 4 + ql
                                  if kt > qsub:
                                      continue
                                  delta = kt - qsub - 1
                                  ins = sc.activation(
                                      ebuf[:, it % 2, kt * 512 + ql * 128:kt * 512 + (ql + 1) * 128],
                                      psS[gs % 2][:, ql * 128:(ql + 1) * 128], AF.Exp,
                                      bias=jtab[:, (gr * 4 + hl) * 32 + 32 + delta:
                                                (gr * 4 + hl) * 32 + 33 + delta])
                              ins.then_inc(S["e_kt"], 1)
                          if it % 32 >= 1:
                              att_evac(it - 1)
              att_evac(ITS - 1)

          @blk.vector
          def _(v):
              for it in range(ITS):
                  qg = it % 8
                  km = kmax[it]
                  for r in range(4):
                      kt = 4 * qg + r
                      gs = cumS[it] + kt
                      v.wait_ge(S["e_kt"], gs + 1)
                      if r > 0:
                          edge2(v, v.memset(ebuf[:, it % 2, kt * 512:kt * 512 + r * 128], 0.0))
                      v.tensor_tensor(ebuf[:, it % 2, kt * 512 + r * 128:kt * 512 + (r + 1) * 128],
                                      ebuf[:, it % 2, kt * 512 + r * 128:kt * 512 + (r + 1) * 128],
                                      diagkeep[:], ALU.mult).then_inc(S["m_kt"], 1)
                      v.wait_ge(S["m_kt"], 4 * it + r + 1)
                  v.wait_ge(S["z_fin"], it + 1)
                  v.reciprocal(izb[:, it % 2, :], ps_z[:]).then_inc(S["rz"], 1)
                  v.wait_ge(S["rz"], it + 1)
                  ins = None
                  for kt in range(km):
                      ins = v.tensor_tensor(ebuf[:, it % 2, kt * 512:(kt + 1) * 512],
                                            ebuf[:, it % 2, kt * 512:(kt + 1) * 512],
                                            izb[:, it % 2, :], ALU.mult)
                  edge2(v, ins)
                  v.tensor_scalar(ebuf[:, it % 2, 0:km * 512],
                                  ebuf[:, it % 2, 0:km * 512],
                                  CLAMP, None, ALU.max).then_inc(S["nrm"], 1)



    def phase3():
      p3_sems = ["ow_ld", "brow_ld", "ab0", "ab1", "x3_0", "x3_1", "p3mm",
                 "p3ev", "p3tr", "p3o1", "p3sqa", "p3lna", "p3v1", "p3x2",
                 "p3v2", "p3xn", "p3xf", "p3xs0", "p3xs1", "p3ns0", "p3ns1"]
      for n in p3_sems:
          S[n] = nc.alloc_semaphore(n)

      with (
          nc.sbuf_tensor("p3_ow", [128, NKD * NKD * 128], BF16) as owt,
          nc.sbuf_tensor("p3_ab", [128, 2, NKD * 128], BF16) as attb,
          nc.sbuf_tensor("p3_of", [128, 2, D], BF16) as of1,
          nc.sbuf_tensor("p3_o1tm", [128, 2, D], BF16) as o1tm,
          nc.sbuf_tensor("p3_x", [128, 2, D], F32) as xb3,
          nc.sbuf_tensor("p3_x2", [128, 2, D], F32) as x2b,
          nc.sbuf_tensor("p3_xn2", [128, 2, D], BF16) as xn2tm,
          nc.sbuf_tensor("p3_xns", [128, 2, D], BF16) as xn2st,
          nc.sbuf_tensor("p3_st", [128, 2, 16], F32) as st3,
          nc.sbuf_tensor("p3_sq", [128, D], F32) as sq3,
          nc.sbuf_tensor("p3_brow", [1, D], F32) as brow,
          nc.psum_tensor("p3_b0", [128, 512], F32) as p3b0,
          nc.psum_tensor("p3_b1", [128, 512], F32) as p3b1,
          nc.psum_tensor("p3_b2", [128, 512], F32) as p3b2,
          nc.psum_tensor("p3_b3", [128, 512], F32) as p3b3,
          nc.psum_tensor("p3_t0", [128, 2048], BF16) as p3t0,
          nc.psum_tensor("p3_t1", [128, 2048], BF16) as p3t1,
          nc.Block() as blk,
      ):
          psB = [p3b0, p3b1, p3b2, p3b3]
          psT = [p3t0, p3t1]

          @blk.gpsimd
          def _(g):
              for m in range(NKD):
                  g.dma_start(owt[:, m * 2048:(m + 1) * 2048],
                              OWT3.ap()[m].rearrange("p kc ml -> p (kc ml)")
                              ).then_inc(S["ow_ld"], 16)
              g.dma_start(brow[0:1, :],
                          owb.ap().rearrange("(o n) -> o n", o=1)).then_inc(S["brow_ld"], 16)
              for ts in range(NTT):
                  if ts >= 2:
                      g.wait_ge(S["p3mm"], NKD * (ts - 1))
                  g.dma_start(attb[:, ts % 2, :].rearrange("p (c n) -> p c n", n=128),
                              ATT.ap()[:, :, ts * 128:(ts + 1) * 128]
                              .rearrange("c p n -> p c n")).then_inc(S["ab%d" % (ts % 2)], 16)

          @blk.sync
          def _(sync):
              for ts in range(NTT):
                  if ts >= 2:
                      sync.wait_ge(S["p3x2"], ts - 1)
                  sync.dma_start(xb3[:, ts % 2, :],
                                 x_in.ap()[ts * 128:(ts + 1) * 128, :]
                                 ).then_inc(S["x3_%d" % (ts % 2)], 16)
                  sync.wait_ge(S["p3x2"], ts + 1)
                  sync.dma_start(X2S.ap()[ts * 128:(ts + 1) * 128, :],
                                 x2b[:, ts % 2, :]).then_inc(S["p3xs%d" % (ts % 2)], 16)
                  sync.wait_ge(S["p3xf"], ts + 1)
                  sync.dma_start(XN2S.ap()[:, :, ts * 128:(ts + 1) * 128]
                                 .rearrange("c p n -> p c n"),
                                 xn2st[:, ts % 2, :].rearrange("p (c n) -> p c n", n=128)
                                 ).then_inc(S["p3ns%d" % (ts % 2)], 16)
              sync.wait_ge(S["p3xs0"], 16 * (NTT // 2))
              sync.wait_ge(S["p3xs1"], 16 * (NTT // 2))
              sync.wait_ge(S["p3ns0"], 16 * (NTT // 2))
              sync.wait_ge(S["p3ns1"], 16 * (NTT // 2))

          @blk.tensor
          def _(t):
              t.wait_ge(S["brow_ld"], 16)
              t.wait_ge(S["ow_ld"], 16 * NKD)
              for ts in range(NTT):
                  t.wait_ge(S["ab%d" % (ts % 2)], 16 * (ts // 2 + 1))
                  for m in range(NKD):
                      bk, q = m // 4, m % 4
                      if q == 0 and ts >= 1:
                          t.wait_ge(S["p3ev"], 4 * ts + bk - 3)
                      bank = psB[bk]
                      t.matmul(bank[:, q * 128:(q + 1) * 128],
                               brow[0:1, m * 128:(m + 1) * 128],
                               onesrow[0:1, 0:128], start=True, stop=False)
                      ins = None
                      for kc in range(NKD):
                          ins = t.matmul(bank[:, q * 128:(q + 1) * 128],
                                         owt[:, m * 2048 + kc * 128:m * 2048 + (kc + 1) * 128],
                                         attb[:, ts % 2, kc * 128:(kc + 1) * 128],
                                         start=False, stop=(kc == NKD - 1))
                      ins.then_inc(S["p3mm"], 1)
                  t.wait_ge(S["p3ev"], 4 * ts + 4)
                  if ts >= 2:
                      t.wait_ge(S["p3xf"], ts - 1)
                  ins = None
                  for m in range(NKD):
                      ins = t.transpose(psT[ts % 2][:, m * 128:(m + 1) * 128],
                                        of1[:, ts % 2, m * 128:(m + 1) * 128], identbf[:])
                  ins.then_inc(S["p3tr"], 1)
                  if ts >= 1:
                      pt = ts - 1
                      t.wait_ge(S["p3xn"], pt + 1)
                      t.wait_ge(S["p3o1"], pt + 1)
                      ins = None
                      for m in range(NKD):
                          ins = t.transpose(psT[pt % 2][:, m * 128:(m + 1) * 128],
                                            xn2tm[:, pt % 2, m * 128:(m + 1) * 128], identbf[:])
                      ins.then_inc(S["p3tr"], 1)
              pt = NTT - 1
              t.wait_ge(S["p3xn"], pt + 1)
              t.wait_ge(S["p3o1"], pt + 1)
              ins = None
              for m in range(NKD):
                  ins = t.transpose(psT[pt % 2][:, m * 128:(m + 1) * 128],
                                    xn2tm[:, pt % 2, m * 128:(m + 1) * 128], identbf[:])
              ins.then_inc(S["p3tr"], 1)

          def o1cnt(ts):
              return 1 if ts == 0 else 2 * ts

          def xncnt(k):
              return 2 * k + 3 if k < NTT - 1 else 2 * NTT

          @blk.scalar
          def _(sc):
              def xn2_evac(pt):
                  sc.wait_ge(S["p3tr"], xncnt(pt))
                  if pt >= 2:
                      sc.wait_ge(S["p3ns%d" % (pt % 2)], 16 * ((pt - 2) // 2 + 1))
                  sc.copy(xn2st[:, pt % 2, :], psT[pt % 2][:, :]).then_inc(S["p3xf"], 1)

              for ts in range(NTT):
                  b = ts % 2
                  for bk in range(4):
                      sc.wait_ge(S["p3mm"], ts * NKD + 4 * (bk + 1))
                      if ts >= 2:
                          sc.wait_ge(S["p3tr"], o1cnt(ts - 2))
                      sc.copy(of1[:, b, bk * 512:(bk + 1) * 512],
                              psB[bk][:, 0:512]).then_inc(S["p3ev"], 1)
                  sc.wait_ge(S["p3tr"], o1cnt(ts))
                  if ts >= 2:
                      sc.wait_ge(S["p3x2"], ts - 1)
                  sc.copy(o1tm[:, b, :], psT[ts % 2][:, :]).then_inc(S["p3o1"], 1)
                  if ts >= 1:
                      xn2_evac(ts - 1)
                  sc.wait_ge(S["p3o1"], ts + 1)
                  sc.activation(sq3[:], o1tm[:, b, :], AF.Square,
                                accum_out=st3[:, b, 10:11]).then_inc(S["p3sqa"], 1)
                  sc.wait_ge(S["p3v1"], ts + 1)
                  sc.activation(st3[:, b, 8:9], st3[:, b, 2:3], AF.Sqrt,
                                bias=epscol[:, 0:1], scale=1.0 / D).then_inc(S["p3lna"], 1)
                  sc.wait_ge(S["p3x2"], ts + 1)
                  sc.activation(sq3[:], x2b[:, b, :], AF.Square,
                                accum_out=st3[:, b, 12:13]).then_inc(S["p3sqa"], 1)
                  sc.wait_ge(S["p3v2"], ts + 1)
                  sc.activation(st3[:, b, 9:10], st3[:, b, 6:7], AF.Sqrt,
                                bias=epscol[:, 0:1], scale=1.0 / D).then_inc(S["p3lna"], 1)
              xn2_evac(NTT - 1)

          @blk.vector
          def _(v):
              for ts in range(NTT):
                  b = ts % 2
                  v.wait_ge(S["p3o1"], ts + 1)
                  edge2(v, v.reduce_sum(st3[:, b, 0:1], o1tm[:, b, :], AX.X))
                  edge2(v, v.tensor_scalar(st3[:, b, 1:2], st3[:, b, 0:1], -1.0 / D,
                                           None, ALU.mult))
                  edge2(v, v.scalar_tensor_tensor(st3[:, b, 3:4], st3[:, b, 0:1], -1.0 / D,
                                                  st3[:, b, 0:1], ALU.mult, ALU.mult))
                  v.wait_ge(S["p3sqa"], 2 * ts + 1)
                  v.tensor_add(st3[:, b, 2:3], st3[:, b, 10:11],
                               st3[:, b, 3:4]).then_inc(S["p3v1"], 1)
                  v.wait_ge(S["p3lna"], 2 * ts + 1)
                  edge2(v, v.reciprocal(st3[:, b, 4:5], st3[:, b, 8:9]))
                  v.wait_ge(S["x3_%d" % b], 16 * (ts // 2 + 1))
                  if ts >= 2:
                      v.wait_ge(S["p3xs%d" % b], 16 * ((ts - 2) // 2 + 1))
                  edge2(v, v.tensor_scalar(x2b[:, b, :], o1tm[:, b, :],
                                           st3[:, b, 1:2], st3[:, b, 4:5],
                                           ALU.add, ALU.mult))
                  v.tensor_add(x2b[:, b, :], x2b[:, b, :],
                               xb3[:, b, :]).then_inc(S["p3x2"], 1)
                  v.wait_ge(S["p3x2"], ts + 1)
                  edge2(v, v.reduce_sum(st3[:, b, 5:6], x2b[:, b, :], AX.X))
                  edge2(v, v.tensor_scalar(st3[:, b, 7:8], st3[:, b, 5:6], -1.0 / D,
                                           None, ALU.mult))
                  edge2(v, v.scalar_tensor_tensor(st3[:, b, 11:12], st3[:, b, 5:6], -1.0 / D,
                                                  st3[:, b, 5:6], ALU.mult, ALU.mult))
                  v.wait_ge(S["p3sqa"], 2 * ts + 2)
                  v.tensor_add(st3[:, b, 6:7], st3[:, b, 12:13],
                               st3[:, b, 11:12]).then_inc(S["p3v2"], 1)
                  v.wait_ge(S["p3lna"], 2 * ts + 2)
                  edge2(v, v.reciprocal(st3[:, b, 13:14], st3[:, b, 9:10]))
                  if ts >= 2:
                      v.wait_ge(S["p3tr"], xncnt(ts - 2))
                  v.tensor_scalar(xn2tm[:, b, :], x2b[:, b, :],
                                  st3[:, b, 7:8], st3[:, b, 13:14],
                                  ALU.add, ALU.mult).then_inc(S["p3xn"], 1)


    def phase3b():
      strips = []
      for rc in range(NFF):
          strips.append((0, rc, D, w1, W1T3))
      for rc in range(NFF):
          strips.append((1, rc, D, w2, W2T3))
      for rc in range(NKD):
          strips.append((2, rc, FF, w3, W3T3))
      NSTR = len(strips)
      items = []
      for i, (wsel, rc, cn, _, _) in enumerate(strips):
          ntile = cn // 128
          nit = (ntile + 7) // 8
          for il in range(nit):
              items.append((i, il, min(8, ntile - il * 8)))
      NIT = len(items)
      first_item = {}
      last_item = {}
      for gidx, (i, il, nt) in enumerate(items):
          if i not in first_item:
              first_item[i] = gidx
          last_item[i] = gidx

      p3b_sems = ["qw0", "qw1", "qd", "qp", "qa", "qs0", "qs1"]
      for n in p3b_sems:
          S[n] = nc.alloc_semaphore(n)

      with (
          nc.sbuf_tensor("q_w", [128, 2, FF], F32) as qwb,
          nc.sbuf_tensor("q_neg", [128, FF], F32) as qneg,
          nc.sbuf_tensor("q_out", [128, 2, FF], BF16) as qout,
          nc.sbuf_tensor("q_stage", [128, 2, FF], BF16) as qstage,
          nc.psum_tensor("q_t0", [128, 1024], BF16) as qt0,
          nc.psum_tensor("q_t1", [128, 1024], BF16) as qt1,
          nc.Block() as blk,
      ):
          psQ = [qt0, qt1]

          @blk.gpsimd
          def _(g):
              for i in range(NSTR + 2):
                  if i < NSTR:
                      wsel, rc, cn, srct, dstt = strips[i]
                      if i >= 2:
                          g.wait_ge(S["qd"], i - 1)
                      g.dma_start(qwb[:, i % 2, 0:cn],
                                  srct.ap()[rc * 128:(rc + 1) * 128, :]
                                  ).then_inc(S["qw%d" % (i % 2)], 16)
                  if i >= 2:
                      j = i - 2
                      wsel, rc, cn, srct, dstt = strips[j]
                      g.wait_ge(S["qa"], last_item[j] + 1)
                      g.dma_start(dstt.ap()[rc].rearrange("p kc ml -> p (kc ml)"),
                                  qstage[:, j % 2, 0:cn]).then_inc(S["qs%d" % (j % 2)], 16)
              g.wait_ge(S["qs0"], 16 * ((NSTR + 1) // 2))
              g.wait_ge(S["qs1"], 16 * (NSTR // 2))

          @blk.vector
          def _(v):
              for i, (wsel, rc, cn, srct, dstt) in enumerate(strips):
                  v.wait_ge(S["qw%d" % (i % 2)], 16 * (i // 2 + 1))
                  if i >= 2:
                      v.wait_ge(S["qp"], last_item[i - 2] + 1)
                  edge2(v, v.tensor_scalar(qneg[:, 0:cn], qwb[:, i % 2, 0:cn],
                                           gthr[:, 2 * wsel + 1:2 * wsel + 2], None,
                                           ALU.is_lt))
                  v.scalar_tensor_tensor(qout[:, i % 2, 0:cn], qwb[:, i % 2, 0:cn],
                                         gthr[:, 2 * wsel:2 * wsel + 1], qneg[:, 0:cn],
                                         ALU.is_gt, ALU.subtract).then_inc(S["qd"], 1)
                  v.wait_ge(S["qd"], i + 1)

          @blk.tensor
          def _(t):
              for gidx, (i, il, nt) in enumerate(items):
                  t.wait_ge(S["qd"], i + 1)
                  if gidx >= 2:
                      t.wait_ge(S["qa"], gidx - 1)
                  ins = None
                  for j in range(nt):
                      kt = il * 8 + j
                      ins = t.transpose(psQ[gidx % 2][:, j * 128:(j + 1) * 128],
                                        qout[:, i % 2, kt * 128:(kt + 1) * 128], identbf[:])
                  ins.then_inc(S["qp"], 1)

          @blk.scalar
          def _(sc):
              for gidx, (i, il, nt) in enumerate(items):
                  sc.wait_ge(S["qp"], gidx + 1)
                  if i >= 2 and il == 0:
                      sc.wait_ge(S["qs%d" % (i % 2)], 16 * ((i - 2) // 2 + 1))
                  sc.copy(qstage[:, i % 2, il * 1024:il * 1024 + nt * 128],
                          psQ[gidx % 2][:, 0:nt * 128]).then_inc(S["qa"], 1)

    def phase4():
      NSL = 8
      p4_sems = ["xnl0", "xnl1", "w12_0", "w12_1", "w3l0", "w3l1", "x2l0", "x2l1",
                 "f1mm", "f2mm", "f3mm", "sgk", "hidk", "foev", "p4tr", "o3ev",
                 "p4sqa", "p4lna", "p4v", "p4y", "yst0", "yst1"]
      for n in p4_sems:
          S[n] = nc.alloc_semaphore(n)

      with (
          nc.sbuf_tensor("p4_xn", [128, 2, NKD * 512], BF16) as xnsl,
          nc.sbuf_tensor("p4_w12", [128, 2, 4 * NKD * 128], BF16) as wst,
          nc.sbuf_tensor("p4_w3", [128, 2, NFF * 128], BF16) as w3st,
          nc.sbuf_tensor("p4_sg", [128, NFF * 512], BF16) as sg,
          nc.sbuf_tensor("p4_fo", [128, 1, NKD * 512], BF16) as fo,
          nc.sbuf_tensor("p4_x2", [128, 2, D], F32) as x2r,
          nc.sbuf_tensor("p4_o3", [128, 2, D], BF16) as o3tm,
          nc.sbuf_tensor("p4_y", [128, 2, D], F32) as ybuf,
          nc.sbuf_tensor("p4_st", [128, 2, 8], F32) as st4,
          nc.sbuf_tensor("p4_sq", [128, D], F32) as sq4,
          nc.psum_tensor("p4_f0", [128, 512], F32) as p4f0,
          nc.psum_tensor("p4_f1", [128, 512], F32) as p4f1,
          nc.psum_tensor("p4_f2", [128, 512], F32) as p4f2,
          nc.psum_tensor("p4_f3", [128, 512], F32) as p4f3,
          nc.psum_tensor("p4_t0", [128, 2048], BF16) as p4t0,
          nc.psum_tensor("p4_t1", [128, 2048], BF16) as p4t1,
          nc.Block() as blk,
      ):
          psF = [p4f0, p4f1, p4f2, p4f3]
          psT4 = [p4t0, p4t1]
          NFG = 11

          @blk.gpsimd
          def _(g):
              for ts in range(NSL):
                  if ts >= 2:
                      g.wait_ge(S["f2mm"], NFF * (ts - 1))
                  g.dma_start(xnsl[:, ts % 2, :].rearrange("p (c n) -> p c n", n=512),
                              XN2S.ap()[:, :, ts * 512:(ts + 1) * 512]
                              .rearrange("c p n -> p c n")).then_inc(S["xnl%d" % (ts % 2)], 16)
                  for wi in range(2):
                      base = ts * 2 * NFG + wi * NFG
                      for fg in range(NFG):
                          gl = base + fg
                          if gl >= 2:
                              pts, prem = (gl - 2) // (2 * NFG), (gl - 2) % (2 * NFG)
                              pwi, pfg = prem // NFG, prem % NFG
                              g.wait_ge(S["f1mm" if pwi == 0 else "f2mm"],
                                        pts * NFF + 4 * pfg + 4)
                          src = (W1T3 if wi == 0 else W2T3)
                          g.dma_start(wst[:, gl % 2, :]
                                      .rearrange("p (f kc ml) -> p f kc ml", f=4, kc=NKD),
                                      src.ap()[4 * fg:4 * fg + 4]
                                      .rearrange("f p kc ml -> p f kc ml")
                                      ).then_inc(S["w12_%d" % (gl % 2)], 16)
                  for d in range(NKD):
                      gl3 = ts * NKD + d
                      if gl3 >= 2:
                          g.wait_ge(S["f3mm"], gl3 - 1)
                      g.dma_start(w3st[:, gl3 % 2, :],
                                  W3T3.ap()[d].rearrange("p kc ml -> p (kc ml)")
                                  ).then_inc(S["w3l%d" % (gl3 % 2)], 16)

          @blk.sync
          def _(sync):
              for ts in range(NSL):
                  for u in range(4):
                      tu = ts * 4 + u
                      if tu >= 2:
                          sync.wait_ge(S["p4y"], tu - 1)
                      sync.dma_start(x2r[:, tu % 2, :],
                                     X2S.ap()[tu * 128:(tu + 1) * 128, :]
                                     ).then_inc(S["x2l%d" % (tu % 2)], 16)
                      sync.wait_ge(S["p4y"], tu + 1)
                      sync.dma_start(y_out.ap()[tu * 128:(tu + 1) * 128, :],
                                     ybuf[:, tu % 2, :]).then_inc(S["yst%d" % (tu % 2)], 16)
              sync.wait_ge(S["yst0"], 16 * (NTT // 2))
              sync.wait_ge(S["yst1"], 16 * (NTT // 2))

          @blk.tensor
          def _(t):
              for ts in range(NSL):
                  t.wait_ge(S["xnl%d" % (ts % 2)], 16 * (ts // 2 + 1))
                  for fg in range(NFG):
                      gl = ts * 2 * NFG + fg
                      t.wait_ge(S["w12_%d" % (gl % 2)], 16 * (gl // 2 + 1))
                      for fl in range(4):
                          f = 4 * fg + fl
                          gf = ts * NFF + f
                          if ts >= 1 and f == 0:
                              t.wait_ge(S["foev"], ts * NKD)
                          if gf >= 4:
                              t.wait_ge(S["sgk"], gf - 3)
                          ins = None
                          for kc in range(NKD):
                              ins = t.matmul(psF[f % 4][:],
                                             wst[:, gl % 2, (fl * NKD + kc) * 128:(fl * NKD + kc + 1) * 128],
                                             xnsl[:, ts % 2, kc * 512:(kc + 1) * 512],
                                             start=(kc == 0), stop=(kc == NKD - 1))
                          ins.then_inc(S["f1mm"], 1)
                  for fg in range(NFG):
                      gl = ts * 2 * NFG + NFG + fg
                      t.wait_ge(S["w12_%d" % (gl % 2)], 16 * (gl // 2 + 1))
                      for fl in range(4):
                          f = 4 * fg + fl
                          if f == 0:
                              t.wait_ge(S["sgk"], (ts + 1) * NFF)
                          if f >= 4:
                              t.wait_ge(S["hidk"], ts * NFF + f - 3)
                          ins = None
                          for kc in range(NKD):
                              ins = t.matmul(psF[f % 4][:],
                                             wst[:, gl % 2, (fl * NKD + kc) * 128:(fl * NKD + kc + 1) * 128],
                                             xnsl[:, ts % 2, kc * 512:(kc + 1) * 512],
                                             start=(kc == 0), stop=(kc == NKD - 1))
                          ins.then_inc(S["f2mm"], 1)
                  for d in range(NKD):
                      gl3 = ts * NKD + d
                      t.wait_ge(S["w3l%d" % (gl3 % 2)], 16 * (gl3 // 2 + 1))
                      if d == 0:
                          t.wait_ge(S["hidk"], (ts + 1) * NFF)
                      if ts * NKD + d >= 4:
                          t.wait_ge(S["foev"], ts * NKD + d - 3)
                      ins = None
                      for fc in range(NFF):
                          ins = t.matmul(psF[d % 4][:],
                                         w3st[:, gl3 % 2, fc * 128:(fc + 1) * 128],
                                         sg[:, fc * 512:(fc + 1) * 512],
                                         start=(fc == 0), stop=(fc == NFF - 1))
                      ins.then_inc(S["f3mm"], 1)
                  for u in range(4):
                      tu = ts * 4 + u
                      t.wait_ge(S["foev"], ts * NKD + NKD)
                      if tu >= 2:
                          t.wait_ge(S["o3ev"], tu - 1)
                      ins = None
                      for c in range(NKD):
                          ins = t.transpose(psT4[tu % 2][:, c * 128:(c + 1) * 128],
                                            fo[:, 0, c * 512 + u * 128:c * 512 + u * 128 + 128],
                                            identbf[:])
                      ins.then_inc(S["p4tr"], 1)

          @blk.scalar
          def _(sc):
              for ts in range(NSL):
                  for f in range(NFF):
                      sc.wait_ge(S["f1mm"], ts * NFF + f + 1)
                      sc.activation(sg[:, f * 512:(f + 1) * 512], psF[f % 4][:],
                                    AF.Sigmoid).then_inc(S["sgk"], 1)
                  for d in range(NKD):
                      sc.wait_ge(S["f3mm"], ts * NKD + d + 1)
                      if ts >= 1 and d == 0:
                          sc.wait_ge(S["p4tr"], 4 * ts)
                      sc.copy(fo[:, 0, d * 512:(d + 1) * 512],
                              psF[d % 4][:]).then_inc(S["foev"], 1)
                  for u in range(4):
                      tu = ts * 4 + u
                      sc.wait_ge(S["p4tr"], tu + 1)
                      if tu >= 2:
                          sc.wait_ge(S["p4y"], tu - 1)
                      sc.copy(o3tm[:, tu % 2, :], psT4[tu % 2][:, :]).then_inc(S["o3ev"], 1)
                      sc.wait_ge(S["o3ev"], tu + 1)
                      sc.activation(sq4[:], o3tm[:, tu % 2, :], AF.Square,
                                    accum_out=st4[:, tu % 2, 0:1]).then_inc(S["p4sqa"], 1)
                      sc.wait_ge(S["p4v"], tu + 1)
                      sc.activation(st4[:, tu % 2, 4:5], st4[:, tu % 2, 3:4], AF.Sqrt,
                                    bias=epscol[:, 0:1], scale=1.0 / D).then_inc(S["p4lna"], 1)

          @blk.vector
          def _(v):
              for ts in range(NSL):
                  for f in range(NFF):
                      v.wait_ge(S["f2mm"], ts * NFF + f + 1)
                      v.wait_ge(S["sgk"], ts * NFF + f + 1)
                      v.tensor_tensor(sg[:, f * 512:(f + 1) * 512], psF[f % 4][:],
                                      sg[:, f * 512:(f + 1) * 512],
                                      ALU.mult).then_inc(S["hidk"], 1)
                  for u in range(4):
                      tu = ts * 4 + u
                      b = tu % 2
                      v.wait_ge(S["o3ev"], tu + 1)
                      edge2(v, v.reduce_sum(st4[:, b, 1:2], o3tm[:, b, :], AX.X))
                      edge2(v, v.tensor_scalar(st4[:, b, 2:3], st4[:, b, 1:2], -1.0 / D,
                                               None, ALU.mult))
                      edge2(v, v.scalar_tensor_tensor(st4[:, b, 5:6], st4[:, b, 1:2],
                                                      -1.0 / D, st4[:, b, 1:2],
                                                      ALU.mult, ALU.mult))
                      v.wait_ge(S["p4sqa"], tu + 1)
                      v.tensor_add(st4[:, b, 3:4], st4[:, b, 0:1],
                                   st4[:, b, 5:6]).then_inc(S["p4v"], 1)
                      v.wait_ge(S["p4lna"], tu + 1)
                      edge2(v, v.reciprocal(st4[:, b, 6:7], st4[:, b, 4:5]))
                      v.wait_ge(S["x2l%d" % b], 16 * (tu // 2 + 1))
                      if tu >= 2:
                          v.wait_ge(S["yst%d" % b], 16 * ((tu - 2) // 2 + 1))
                      edge2(v, v.tensor_scalar(ybuf[:, b, :], o3tm[:, b, :],
                                               st4[:, b, 2:3], st4[:, b, 6:7],
                                               ALU.add, ALU.mult))
                      v.tensor_add(ybuf[:, b, :], ybuf[:, b, :],
                                   x2r[:, b, :]).then_inc(S["p4y"], 1)

    if 0 in phases:
        phase0()
    if 1 in phases:
        phase1()
    if 2 in phases:
        phase2()
    if 3 in phases:
        phase3()
    if 35 in phases:
        phase3b()
    if 4 in phases:
        phase4()

    if debug_taps:
        dbg_sem = nc.alloc_semaphore("dbg_sem")
        with nc.Block() as blk:
            outs = []
            for name in debug_taps:
                src = scratch[name]
                dt = nc.dram_tensor("dbg_" + name, list(src.shape), src.dtype,
                                    kind="ExternalOutput")
                outs.append((dt, src))

            @blk.gpsimd
            def _(g):
                n = 0
                for dt, src in outs:
                    g.dma_start(dt.ap(), src.ap()).then_inc(dbg_sem, 16)
                    n += 1
                g.wait_ge(dbg_sem, 16 * n)

    return nc


def host_consts():
    ii = np.arange(128)
    dbig = (ii[:, None] + 128.0 * (np.arange(32)[None, :] - 32.0)).astype(np.float32)
    diagkeep = (ii[:, None] <= ii[None, :]).astype(BF)
    return {
        "c_identbf": np.eye(128, dtype=np.float32).astype(BF),
        "c_identf": np.eye(128, dtype=np.float32),
        "c_dbig": dbig,
        "c_diagkeep": diagkeep,
    }


def make_in_maps(L0_attn_qkv, L0_attn_in_proj, L0_attn_in_proj_bias, L0_attn_out_w,
                 L0_attn_out_b, L0_attn_alibi_slopes, L0_ffn_w1, L0_ffn_w2, L0_ffn_w3):
    consts = host_consts()
    in_maps = []
    for b in range(8):
        m = {
            "x": np.ascontiguousarray(np.asarray(L0_attn_qkv)[b]),
            "ipw": np.asarray(L0_attn_in_proj),
            "ipb": np.asarray(L0_attn_in_proj_bias),
            "oww": np.asarray(L0_attn_out_w),
            "owb": np.asarray(L0_attn_out_b),
            "alibi": np.asarray(L0_attn_alibi_slopes),
            "w1": np.asarray(L0_ffn_w1),
            "w2": np.asarray(L0_ffn_w2),
            "w3": np.asarray(L0_ffn_w3),
        }
        m.update(consts)
        in_maps.append(m)
    return in_maps


def kernel(**inputs):
    nc = build(DEBUG_TAPS)
    in_maps = make_in_maps(**inputs)
    res = run_bass_kernel_spmd(nc, in_maps, core_ids=list(range(8)))
    return np.stack([r["y"] for r in res.results], axis=0)
